# version 20
# baseline (speedup 1.0000x reference)
from contextlib import ExitStack

import numpy as np
import concourse.bass as bass
import concourse.mybir as mybir
from concourse.bass_utils import run_bass_kernel_spmd

F32 = mybir.dt.float32
BF16 = mybir.dt.bfloat16
I32 = mybir.dt.int32
I16 = mybir.dt.int16
AF = mybir.ActivationFunctionType
ALU = mybir.AluOpType
AXX = mybir.AxisListType.X

P = 128
D = 2048
SEQ = 8192
NCORE = 8
T = 1024
NT = T // P
NH = D // P
INTER = 5504
NI = INTER // P
GSH = D // NCORE
DSH = INTER // NCORE
HALVES = [(0, 22), (22 * 128, 21)]
SCALE = float(1.0 / np.sqrt(128.0))
EPS = 1e-5
ENGS = ["sync", "scalar", "vector", "tensor", "gpsimd"]


class Ctr:
    def __init__(self, stack, nc, name, registry):
        self.sem = stack.enter_context(nc.semaphore(name))
        self.n = 0
        registry[name] = self

    def bump(self, k=1):
        self.n += k
        return self.n


class Emitter:
    def __init__(self):
        self.q = {n: [] for n in ENGS}
        self.blk = None

    def set_block(self, blk):
        self.blk = blk

    def on(self, name, fn):
        self.q[name].append(fn)

    def flush(self):
        for n in ENGS:
            fns = self.q[n]
            if fns:
                getattr(self.blk, n)(lambda e, fns=tuple(fns): [f(e) for f in fns])
                self.q[n] = []


def build_nc(stop_stage="ffn2"):
    STAGES = ["p0", "ln1", "gateup", "ffn1", "proj", "flash", "oproj", "ffn2"]
    _st = {"dumpw": "p0", "dumpln": "ln1", "dumph": "gateup"}.get(stop_stage, stop_stage)
    stop = STAGES.index(_st)
    dumpw = stop_stage == "dumpw"
    dumpln = stop_stage == "dumpln"
    dumph = stop_stage == "dumph"

    nc = bass.Bass("TRN2", target_bir_lowering=False)

    x_in = nc.dram_tensor("x", [T, D], F32, kind="ExternalInput").ap()
    w_in = {m: nc.dram_tensor("w" + m, [D, D], F32, kind="ExternalInput").ap()
            for m in "qkvo"}
    b_in = {m: nc.dram_tensor("b" + m, [D], F32, kind="ExternalInput").ap()
            for m in "qkvo"}
    lut_in = nc.dram_tensor("lut", [16], F32, kind="ExternalInput").ap()
    gidx_in = nc.dram_tensor("gidx", [INTER, GSH], I32, kind="ExternalInput").ap()
    uidx_in = nc.dram_tensor("uidx", [INTER, GSH], I32, kind="ExternalInput").ap()
    didx_in = nc.dram_tensor("didx", [D, DSH], I32, kind="ExternalInput").ap()
    y_out = nc.dram_tensor("y", [T, D], F32, kind="ExternalOutput").ap()
    if dumpln:
        dln = nc.dram_tensor("dln", [P, NH * T], BF16, kind="ExternalOutput").ap()
    if dumph:
        dh = nc.dram_tensor("dh", [P, 22 * T], BF16, kind="ExternalOutput").ap()
    if dumpw:
        dw_g = nc.dram_tensor("dw_g", [D, INTER], BF16, kind="ExternalOutput").ap()
        dw_u = nc.dram_tensor("dw_u", [D, INTER], BF16, kind="ExternalOutput").ap()
        dw_d = nc.dram_tensor("dw_d", [INTER, D], BF16, kind="ExternalOutput").ap()
        dw_q = nc.dram_tensor("dw_q", [D, D], BF16, kind="ExternalOutput").ap()

    wg_sh = nc.dram_tensor("wg_sh", [GSH, INTER], BF16)
    wu_sh = nc.dram_tensor("wu_sh", [GSH, INTER], BF16)
    wd_sh = nc.dram_tensor("wd_sh", [DSH, D], BF16)
    wgT = nc.dram_tensor("wgT", [D, INTER], BF16, addr_space="Shared")
    wuT = nc.dram_tensor("wuT", [D, INTER], BF16, addr_space="Shared")
    wdT = nc.dram_tensor("wdT", [INTER, D], BF16, addr_space="Shared")
    wnat = {m: nc.dram_tensor("wnat" + m, [D, D], BF16) for m in "qkvo"}
    kT_loc = nc.dram_tensor("kT_loc", [D, T], BF16)
    v_loc = nc.dram_tensor("v_loc", [T, D], BF16)
    kT_all = nc.dram_tensor("kT_all", [NCORE * D, T], BF16, addr_space="Shared")
    v_all = nc.dram_tensor("v_all", [SEQ, D], BF16, addr_space="Shared")

    E = Emitter()
    st = ExitStack()
    with st:
        R = {}
        pe = Ctr(st, nc, "pe", R)
        act = Ctr(st, nc, "act", R)
        dve = Ctr(st, nc, "dve", R)
        pool = Ctr(st, nc, "pool", R)
        cc = {k: Ctr(st, nc, "cc_" + k, R) for k in ["wg", "wu", "wd", "kt", "v"]}
        c_idx = Ctr(st, nc, "c_idx", R)
        c_sml = Ctr(st, nc, "c_sml", R)
        c_cast = Ctr(st, nc, "c_cast", R)
        c_x = Ctr(st, nc, "c_x", R)
        c_shst = Ctr(st, nc, "c_shst", R)
        c_wg = [Ctr(st, nc, f"c_wg{i}", R) for i in range(2)]
        c_wu = [Ctr(st, nc, f"c_wu{i}", R) for i in range(2)]
        c_wd = [Ctr(st, nc, f"c_wd{i}", R) for i in range(4)]
        c_wp = [Ctr(st, nc, f"c_wp{i}", R) for i in range(2)]
        c_wv = [Ctr(st, nc, f"c_wv{i}", R) for i in range(4)]
        c_st2 = Ctr(st, nc, "c_st2", R)
        c_k = [Ctr(st, nc, f"c_k{i}", R) for i in range(2)]
        c_v = [Ctr(st, nc, f"c_v{i}", R) for i in range(2)]

        def snap():
            return {k: c.n for k, c in R.items()}

        def mkwaits(bar):
            items = [(R[k].sem, v) for k, v in bar.items()
                     if v > 0 and not k.startswith("cc_")]
            def w(e):
                for sem, v in items:
                    e.wait_ge(sem, v)
            return w

        def gate_phase(bar):
            w = mkwaits(bar)
            for name in ENGS:
                E.on(name, w)

        xres = nc.alloc_sbuf_tensor("xres", [P, NT * D], F32).ap()
        sqscr = nc.alloc_sbuf_tensor("sqscr", [P, D], F32).ap()
        xlntok = nc.alloc_sbuf_tensor("xlntok", [P, D], BF16).ap()
        ident = nc.alloc_sbuf_tensor("ident", [P, P], BF16).ap()
        ones_bf = nc.alloc_sbuf_tensor("ones_bf", [1, 512], BF16).ap()
        onesf = nc.alloc_sbuf_tensor("onesf", [1, P], F32).ap()
        lutrow = nc.alloc_sbuf_tensor("lutrow", [1, 16], F32).ap()
        lutbc = nc.alloc_sbuf_tensor("lutbc", [P, 16], F32).ap()
        brow = {m: nc.alloc_sbuf_tensor("brow" + m, [1, D], BF16).ap() for m in "qkvo"}
        sums = [nc.alloc_sbuf_tensor(f"sums{i}", [P, 1], F32).ap() for i in range(2)]
        sumsq = [nc.alloc_sbuf_tensor(f"sumsq{i}", [P, 1], F32).ap() for i in range(2)]
        mu = [nc.alloc_sbuf_tensor(f"mu{i}", [P, 1], F32).ap() for i in range(2)]
        musq = [nc.alloc_sbuf_tensor(f"musq{i}", [P, 1], F32).ap() for i in range(2)]
        var = [nc.alloc_sbuf_tensor(f"var{i}", [P, 1], F32).ap() for i in range(2)]
        rstd = [nc.alloc_sbuf_tensor(f"rstd{i}", [P, 1], F32).ap() for i in range(2)]
        nmu = [nc.alloc_sbuf_tensor(f"nmu{i}", [P, 1], F32).ap() for i in range(2)]
        xlntok2 = [xlntok, xlntok]

        x_view = x_in.rearrange("(j p) h -> p j h", p=P)
        y_view = y_out.rearrange("(j p) h -> p j h", p=P)
        xres3 = xres.rearrange("p (j h) -> p j h", h=D)

        def emit_store(bar):
            w = mkwaits(bar)
            yc = c_x.bump(16)
            def sfin(s):
                w(s)
                s.dma_start(out=y_view, in_=xres3).then_inc(c_x.sem, 16)
                s.wait_ge(c_x.sem, yc)
            E.on("sync", sfin)

        with ExitStack() as p0:
            idxi = p0.enter_context(nc.sbuf_tensor("idxi", [P, 11008], I16)).ap()
            acc0 = p0.enter_context(nc.sbuf_tensor("acc0", [P, 11008], BF16)).ap()
            acc1 = p0.enter_context(nc.sbuf_tensor("acc1", [P, 11008], BF16)).ap()
            mmsk = p0.enter_context(nc.sbuf_tensor("mmsk", [P, 5504], BF16)).ap()
            rowbuf = p0.enter_context(nc.sbuf_tensor("rowbuf", [P, INTER], BF16)).ap()
            mmsk_p = sqscr.bitcast(BF16)[:, 0:2752]
            ps_lut = p0.enter_context(nc.psum_tensor("ps_lut", [P, 16], F32)).ap()
            ps_t = [p0.enter_context(nc.psum_tensor(f"ps_t{i}", [P, P], BF16)).ap()
                    for i in range(2)]
            blk = p0.enter_context(nc.Block())
            E.set_block(blk)

            idx_g_cnt = c_idx.bump(16)
            sml_cnt = c_sml.bump(16 * 5)
            pool_ones = pool.bump(2)
            pool_ident = pool.bump(2)
            def g0(g):
                g.dma_start(out=idxi.rearrange("p (a h) -> p a h", h=GSH),
                            in_=gidx_in.rearrange("(a p) h -> p a h", p=P)
                            ).then_inc(c_idx.sem, 16)
                g.dma_start(out=lutrow[0:1, :],
                            in_=lut_in.unsqueeze(0)).then_inc(c_sml.sem, 16)
                for m in "qkvo":
                    g.dma_start(out=brow[m][0:1, :],
                                in_=b_in[m].unsqueeze(0)).then_inc(c_sml.sem, 16)
                g.memset(ones_bf[0:1, :], 1.0).then_inc(pool.sem, 1)
                g.memset(onesf[0:1, :], 1.0).then_inc(pool.sem, 1)
                g.memset(ident[:, :], 1.0).then_inc(pool.sem, 1)
                g.wait_ge(pool.sem, pool_ones + 1)
                g.affine_select(ident[:, :], ident[:, :], [[-1, P]],
                                ALU.is_equal, 0.0, base=0,
                                channel_multiplier=1).then_inc(pool.sem, 1)
            E.on("gpsimd", g0)

            x_cnt = c_x.bump(16)
            E.on("sync", lambda s: s.dma_start(out=xres3, in_=x_view)
                 .then_inc(c_x.sem, 16))

            pe_lut = pe.bump()
            def t0(t):
                t.wait_ge(c_sml.sem, sml_cnt)
                t.wait_ge(pool.sem, pool_ones)
                t.matmul(ps_lut, onesf[0:1, :], lutrow[0:1, :], start=True,
                         stop=True, skip_group_check=True).then_inc(pe.sem, 1)
            E.on("tensor", t0)

            dve_lutbc = dve.bump()
            def v0(v):
                v.wait_ge(pe.sem, pe_lut)
                v.tensor_copy(lutbc, ps_lut).then_inc(dve.sem, 1)
            E.on("vector", v0)

            def materialize(accb, idx_cnt, pe_wait=None):
                seq = []
                idxfree = 0
                for tt in range(16):
                    for hh in range(2):
                        sl = slice(5504 * hh, 5504 * (hh + 1))
                        seq.append(("mask", sl, tt, dve.n))
                        dve.bump()
                        if tt > 0:
                            seq.append(("add", sl, tt, dve.n))
                            dve.bump()
                    if tt == 15:
                        idxfree = dve.n
                done = dve.n
                def vm(v, seq=tuple(seq)):
                    v.wait_ge(c_idx.sem, idx_cnt)
                    if pe_wait is not None:
                        v.wait_ge(pe.sem, pe_wait)
                    for kind, sl, tt, wv in seq:
                        v.wait_ge(dve.sem, wv)
                        if kind == "mask":
                            dst = accb[:, sl] if tt == 0 else mmsk
                            v.tensor_scalar(dst, idxi[:, sl], float(tt),
                                            lutbc[:, tt:tt + 1], ALU.is_equal,
                                            ALU.mult).then_inc(dve.sem, 1)
                        else:
                            v.tensor_tensor(accb[:, sl], accb[:, sl], mmsk,
                                            ALU.add).then_inc(dve.sem, 1)
                E.on("vector", vm)
                return (idxfree, 0), (done, 0)

            def emit_transposes(accb, pieces, natiles, piece_stride, done_dve,
                                sh_dram, out_dram, cname, ncols):
                done_d, done_p = done_dve
                k = 0
                for (off, cnt) in pieces:
                    for a in range(natiles):
                        par = k % 2
                        tr_w_act = act.n if k < 2 else act.n - 1
                        tr_tgt = pe.bump()
                        def tt_(t, a=a, off=off, cnt=cnt, par=par, k=k,
                                tr_w_act=tr_w_act):
                            if k == 0:
                                t.wait_ge(dve.sem, done_d)
                                t.wait_ge(pool.sem, max(done_p, pool_ident))
                            if tr_w_act > 0:
                                t.wait_ge(act.sem, tr_w_act)
                            t.transpose(ps_t[par][0:cnt, :],
                                        accb[:, a * piece_stride + off:
                                             a * piece_stride + off + cnt],
                                        ident).then_inc(pe.sem, 1)
                        E.on("tensor", tt_)
                        shst_w = c_shst.n if a == 0 else None
                        act.bump()
                        def ev(sc, a=a, cnt=cnt, par=par, tr_tgt=tr_tgt,
                               shst_w=shst_w):
                            sc.wait_ge(pe.sem, tr_tgt)
                            if shst_w is not None:
                                sc.wait_ge(c_shst.sem, shst_w)
                            sc.copy(rowbuf[0:cnt, 128 * a:128 * a + 128],
                                    ps_t[par][0:cnt, :]).then_inc(act.sem, 1)
                        E.on("scalar", ev)
                        k += 1
                    act_w = act.n
                    c_shst.bump(16)
                    def gst(g, off=off, cnt=cnt, act_w=act_w):
                        g.wait_ge(act.sem, act_w)
                        g.dma_start(out=sh_dram.ap()[off:off + cnt, :],
                                    in_=rowbuf[0:cnt, 0:ncols]
                                    ).then_inc(c_shst.sem, 16)
                    E.on("gpsimd", gst)
                shst_all = c_shst.n
                cc[cname].bump()
                def gag(g, shst_all=shst_all):
                    g.wait_ge(c_shst.sem, shst_all)
                    g.collective_compute(
                        "AllGather", ALU.bypass,
                        replica_groups=[list(range(NCORE))],
                        ins=[sh_dram.ap()],
                        outs=[out_dram.ap()]).then_inc(cc[cname].sem, 1)
                E.on("gpsimd", gag)

            gfree, gdone = materialize(acc0, idx_g_cnt)

            idx_u_cnt = c_idx.bump(16)
            def g1(g, gfree=gfree):
                g.wait_ge(dve.sem, gfree[0])
                if gfree[1] > 0:
                    g.wait_ge(pool.sem, gfree[1])
                g.dma_start(out=idxi.rearrange("p (a h) -> p a h", h=GSH),
                            in_=uidx_in.rearrange("(a p) h -> p a h", p=P)
                            ).then_inc(c_idx.sem, 16)
            E.on("gpsimd", g1)

            ufree, udone = materialize(acc1, idx_u_cnt)
            emit_transposes(acc0, [(0, 128), (128, 128)], NI, GSH, gdone,
                            wg_sh, wgT, "wg", INTER)
            pe_gate_tr = pe.n

            idx_d_cnt = c_idx.bump(16)
            def g2(g, ufree=ufree):
                g.wait_ge(dve.sem, ufree[0])
                if ufree[1] > 0:
                    g.wait_ge(pool.sem, ufree[1])
                g.dma_start(out=idxi.rearrange("p (a i) -> p a i", i=DSH),
                            in_=didx_in.rearrange("(a p) i -> p a i", p=P)
                            ).then_inc(c_idx.sem, 16)
            E.on("gpsimd", g2)

            dfree, ddone = materialize(acc0, idx_d_cnt, pe_wait=pe_gate_tr)
            emit_transposes(acc1, [(0, 128), (128, 128)], NI, GSH, udone,
                            wu_sh, wuT, "wu", INTER)
            dpieces = [(128 * j, 128) for j in range(5)] + [(640, 48)]
            emit_transposes(acc0, dpieces, NH, DSH, ddone,
                            wd_sh, wdT, "wd", D)

            bar_p0 = snap()
            E.flush()

        if stop == 0:
            with nc.Block() as blk:
                E.set_block(blk)
                if dumpw:
                    w0 = mkwaits(bar_p0)
                    tgt = c_shst.bump(64)
                    def gdump(g):
                        w0(g)
                        for ccn in ["wg", "wu", "wd"]:
                            g.wait_ge(cc[ccn].sem, 1)
                        g.dma_start(out=dw_g, in_=wgT.ap()).then_inc(c_shst.sem, 16)
                        g.dma_start(out=dw_u, in_=wuT.ap()).then_inc(c_shst.sem, 16)
                        g.dma_start(out=dw_d, in_=wdT.ap()).then_inc(c_shst.sem, 16)
                        g.wait_ge(c_shst.sem, tgt)
                    E.on("gpsimd", gdump)
                emit_store(bar_p0)
                E.flush()
            return nc

        xlnT = nc.alloc_sbuf_tensor("xlnT", [P, NH * T], BF16).ap()
        attnT = xlnT
        cbuf = nc.alloc_sbuf_tensor("cbuf", [P, 22 * T], BF16).ap()
        hact = cbuf
        qT = cbuf

        ln_prev_aexp = {}
        ln_prev_varread = {}
        ln_prev_cp = {}
        ln_var = {}

        def emit_ln_stats(j, prev_asq=None):
            pr = j % 2
            w_varread = ln_prev_varread.get(j - 2)
            a_sq = act.bump()
            def ssq(sc, j=j, pr=pr, w_varread=w_varread, prev_asq=prev_asq):
                sc.wait_ge(c_x.sem, x_cnt)
                if prev_asq is not None:
                    sc.wait_ge(act.sem, prev_asq)
                if w_varread is not None:
                    sc.wait_ge(dve.sem, w_varread)
                sc.activation(sqscr, xres3[:, j, :], AF.Square,
                              accum_out=sumsq[pr]).then_inc(act.sem, 1)
            E.on("scalar", ssq)

            w0 = dve.n
            s_sum = dve.bump()
            s_mu = dve.bump()
            s_musq = dve.bump()
            s_var = dve.bump()
            s_sub = dve.bump()
            s_eps = dve.bump()
            w_aexp = ln_prev_aexp.get(j - 2)
            def vln(v, j=j, pr=pr, w0=w0, w_aexp=w_aexp, a_sq=a_sq):
                v.wait_ge(c_x.sem, x_cnt)
                v.wait_ge(dve.sem, w0)
                if w_aexp is not None:
                    v.wait_ge(act.sem, w_aexp)
                v.reduce_sum(sums[pr], xres3[:, j, :], axis=AXX).then_inc(dve.sem, 1)
                v.wait_ge(dve.sem, s_sum)
                v.tensor_scalar_mul(mu[pr], sums[pr], 1.0 / D).then_inc(dve.sem, 1)
                v.wait_ge(dve.sem, s_mu)
                v.scalar_tensor_tensor(musq[pr], mu[pr], 1.0, mu[pr], ALU.mult,
                                       ALU.mult).then_inc(dve.sem, 1)
                v.wait_ge(dve.sem, s_musq)
                v.wait_ge(act.sem, a_sq)
                v.tensor_scalar_mul(var[pr], sumsq[pr], 1.0 / D).then_inc(dve.sem, 1)
                v.wait_ge(dve.sem, s_var)
                v.tensor_sub(var[pr], var[pr], musq[pr]).then_inc(dve.sem, 1)
                v.wait_ge(dve.sem, s_sub)
                v.tensor_scalar_add(var[pr], var[pr], EPS).then_inc(dve.sem, 1)
            E.on("vector", vln)
            ln_prev_varread[j] = s_var
            ln_var[j] = s_eps

        def emit_ln_norm(j):
            pr = j % 2
            dve_var = ln_var[j]
            a_ln = act.bump()
            a_exp = act.bump()
            ln_prev_aexp[j] = a_exp
            def scr(sc, pr=pr, dve_var=dve_var, a_ln=a_ln):
                sc.wait_ge(dve.sem, dve_var)
                sc.activation(sums[pr], var[pr], AF.Ln).then_inc(act.sem, 1)
                sc.wait_ge(act.sem, a_ln)
                sc.activation(rstd[pr], sums[pr], AF.Exp,
                              scale=-0.5).then_inc(act.sem, 1)
            E.on("scalar", scr)
            w_pe = pe.n
            s_nmu = dve.bump()
            s_norm = dve.bump()
            def vnm(v, j=j, pr=pr, a_exp=a_exp, w_pe=w_pe, s_nmu=s_nmu):
                v.wait_ge(act.sem, a_exp)
                v.scalar_tensor_tensor(nmu[pr], mu[pr], -1.0, rstd[pr], ALU.mult,
                                       ALU.mult).then_inc(dve.sem, 1)
                v.wait_ge(dve.sem, s_nmu)
                v.wait_ge(pe.sem, w_pe)
                v.tensor_scalar(xlntok2[pr], xres3[:, j, :], rstd[pr], nmu[pr],
                                ALU.mult, ALU.add).then_inc(dve.sem, 1)
            E.on("vector", vnm)
            return s_norm

        def emit_ln_transposes(j, bt, dve_norm):
            pr = j % 2
            cp_act = []
            for g4 in range(4):
                par = g4 % 2
                w_bt = ln_prev_cp.get(j - 1) if g4 < 2 else cp_act[g4 - 2]
                tr_tgt = pe.bump()
                def ttr(t, g4=g4, pr=pr, par=par, w_bt=w_bt, dve_norm=dve_norm):
                    t.wait_ge(dve.sem, dve_norm)
                    if w_bt is not None:
                        t.wait_ge(act.sem, w_bt)
                    for i in range(4):
                        c = g4 * 4 + i
                        inst = t.transpose(bt[par][:, 128 * i:128 * (i + 1)],
                                           xlntok2[pr][:, 128 * c:128 * (c + 1)],
                                           ident)
                        if i == 3:
                            inst.then_inc(pe.sem, 1)
                E.on("tensor", ttr)
                cp_act.append(act.bump())
                def evr(sc, j=j, g4=g4, par=par, tr_tgt=tr_tgt):
                    sc.wait_ge(pe.sem, tr_tgt)
                    sc.copy(
                        xlnT.rearrange("p (c t) -> p c t", t=T)
                            [:, 4 * g4:4 * g4 + 4, 128 * j:128 * (j + 1)],
                        bt[par][:, 0:512].rearrange("p (c t) -> p c t", t=P)
                    ).then_inc(act.sem, 1)
                E.on("scalar", evr)
            ln_prev_cp[j] = cp_act[-1]

        def emit_ln_all(bt):
            for j in range(NT):
                emit_ln_stats(j)
                if j >= 1:
                    dn = emit_ln_norm(j - 1)
                    emit_ln_transposes(j - 1, bt, dn)
            dn = emit_ln_norm(NT - 1)
            emit_ln_transposes(NT - 1, bt, dn)

        def emit_ffn(B, bt, sub="full", tile_hook=None):
            emit_ln_all(bt)
            ln_done = dve.n
            if sub == "ln":
                return

            wgt3 = wgT.ap().rearrange("(c p) i -> p c i", p=P)
            wut3 = wuT.ap().rearrange("(c p) i -> p c i", p=P)
            wbg = B["wbg"]; wbu = B["wbu"]; wbd = B["wbd"]
            psb = B["psb"]

            for (io, na) in HALVES:
                pe_hist = []
                for a in range(na):
                    par = a % 2
                    w_pe = pe.n if a < 2 else pe_hist[a - 2]
                    wg_cnt = c_wg[par].bump(16)
                    wu_cnt = c_wu[par].bump(16)
                    def ldw(s, a=a, par=par, io=io, w_pe=w_pe):
                        if a == 0 and io == 0:
                            s.wait_ge(cc["wg"].sem, 1)
                            s.wait_ge(cc["wu"].sem, 1)
                        s.wait_ge(pe.sem, w_pe)
                        s.dma_start(
                            out=wbg[par].rearrange("p (c i) -> p c i", i=P),
                            in_=wgt3[:, :, io + 128 * a: io + 128 * (a + 1)]
                        ).then_inc(c_wg[par].sem, 16)
                        s.dma_start(
                            out=wbu[par].rearrange("p (c i) -> p c i", i=P),
                            in_=wut3[:, :, io + 128 * a: io + 128 * (a + 1)]
                        ).then_inc(c_wu[par].sem, 16)
                    E.on("sync", ldw)

                    for th in range(2):
                        w_dve_first = dve.n if (a == 0 and th == 0) else None
                        w_act = act.n
                        pe_gate = pe.bump()
                        def mmg(t, a=a, th=th, par=par, wg_cnt=wg_cnt,
                                w_act=w_act, w_dve_first=w_dve_first):
                            if w_dve_first is not None:
                                t.wait_ge(dve.sem, w_dve_first)
                            t.wait_ge(c_wg[par].sem, wg_cnt)
                            t.wait_ge(act.sem, w_act)
                            for c in range(NH):
                                inst = t.matmul(
                                    psb[th], wbg[par][:, 128 * c:128 * (c + 1)],
                                    xlnT[:, c * T + 512 * th: c * T + 512 * (th + 1)],
                                    start=(c == 0), stop=(c == NH - 1),
                                    skip_group_check=True)
                                if c == NH - 1:
                                    inst.then_inc(pe.sem, 1)
                        E.on("tensor", mmg)

                        w_dve = dve.n
                        pe_up = pe.bump()
                        def mmu(t, a=a, th=th, par=par, wu_cnt=wu_cnt, w_dve=w_dve):
                            t.wait_ge(dve.sem, w_dve)
                            t.wait_ge(c_wu[par].sem, wu_cnt)
                            for c in range(NH):
                                inst = t.matmul(
                                    psb[2 + th], wbu[par][:, 128 * c:128 * (c + 1)],
                                    xlnT[:, c * T + 512 * th: c * T + 512 * (th + 1)],
                                    start=(c == 0), stop=(c == NH - 1),
                                    skip_group_check=True)
                                if c == NH - 1:
                                    inst.then_inc(pe.sem, 1)
                        E.on("tensor", mmu)

                        w_vh_prev = dve.n
                        act_silu = act.bump()
                        def scs(sc, th=th, pe_gate=pe_gate, w_vh_prev=w_vh_prev):
                            sc.wait_ge(pe.sem, pe_gate)
                            sc.wait_ge(dve.sem, w_vh_prev)
                            sc.activation(sqscr[:, 512 * th:512 * (th + 1)],
                                          psb[th], AF.Silu).then_inc(act.sem, 1)
                        E.on("scalar", scs)

                        w_dve3 = dve.n
                        dve.bump()
                        def vh(v, a=a, th=th, act_silu=act_silu, pe_up=pe_up,
                               w_dve3=w_dve3):
                            v.wait_ge(pe.sem, pe_up)
                            v.wait_ge(act.sem, act_silu)
                            v.wait_ge(dve.sem, w_dve3)
                            v.tensor_tensor(
                                hact[:, a * T + 512 * th: a * T + 512 * (th + 1)],
                                sqscr[:, 512 * th:512 * (th + 1)],
                                psb[2 + th], ALU.mult).then_inc(dve.sem, 1)
                        E.on("vector", vh)
                    pe_hist.append(pe.n)
                    if tile_hook is not None:
                        tile_hook()
                dve_hact = dve.n
                if sub == "gateup":
                    continue

                prev_vres = None
                for og in range(4):
                    pe_dh = []
                    for a in range(na):
                        w_pe = pe.n if a < 4 else pe_dh[a - 4]
                        wd_cnt = c_wd[a % 4].bump(16)
                        def ldd(s, a=a, og=og, io=io, w_pe=w_pe):
                            if a == 0 and og == 0 and io == 0:
                                s.wait_ge(cc["wd"].sem, 1)
                            s.wait_ge(pe.sem, w_pe)
                            s.dma_start(
                                out=wbd[a % 4],
                                in_=wdT.ap()[io + 128 * a: io + 128 * (a + 1),
                                             512 * og: 512 * (og + 1)]
                            ).then_inc(c_wd[a % 4].sem, 16)
                        E.on("sync", ldd)

                        w_dve = (dve_hact if og == 0 else None) if a == 0 else None
                        pv_w = prev_vres if (a == 0 and og > 0) else None
                        pe.bump()
                        def mmd(t, a=a, og=og, na=na, wd_cnt=wd_cnt,
                                w_dve=w_dve, pv_w=pv_w):
                            if w_dve is not None:
                                t.wait_ge(dve.sem, w_dve)
                            t.wait_ge(c_wd[a % 4].sem, wd_cnt)
                            for tt in range(NT):
                                if pv_w is not None:
                                    t.wait_ge(dve.sem, pv_w[tt])
                                inst = t.matmul(
                                    B["dbank"][tt],
                                    hact[:, a * T + 128 * tt: a * T + 128 * (tt + 1)],
                                    wbd[a % 4],
                                    start=(a == 0), stop=(a == na - 1),
                                    skip_group_check=True)
                                if tt == NT - 1:
                                    inst.then_inc(pe.sem, 1)
                        E.on("tensor", mmd)
                        pe_dh.append(pe.n)
                    pe_down = pe.n

                    w_dve4 = dve.n
                    vres_seq = [dve.bump() for _ in range(NT)]
                    def vres(v, og=og, pe_down=pe_down, w_dve4=w_dve4,
                             vres_seq=tuple(vres_seq)):
                        v.wait_ge(pe.sem, pe_down)
                        v.wait_ge(dve.sem, w_dve4)
                        for tt in range(NT):
                            if tt > 0:
                                v.wait_ge(dve.sem, vres_seq[tt - 1])
                            v.scalar_tensor_tensor(
                                xres3[:, tt, 512 * og:512 * (og + 1)],
                                B["dbank"][tt], 0.5,
                                xres3[:, tt, 512 * og:512 * (og + 1)],
                                ALU.mult, ALU.add).then_inc(dve.sem, 1)
                    E.on("vector", vres)
                    prev_vres = vres_seq

        def make_ffn_bufs(ps, tag):
            wbg = [ps.enter_context(nc.sbuf_tensor(f"wbg{tag}{i}", [P, D], BF16)).ap()
                   for i in range(2)]
            wbu = [ps.enter_context(nc.sbuf_tensor(f"wbu{tag}{i}", [P, D], BF16)).ap()
                   for i in range(2)]
            wbd = [ps.enter_context(nc.sbuf_tensor(f"wbd{tag}{i}", [P, 512], BF16)).ap()
                   for i in range(4)]
            bt = [ps.enter_context(nc.psum_tensor(f"bt{tag}{i}", [P, 1024], BF16)).ap()
                  for i in range(2)]
            psb = [ps.enter_context(nc.psum_tensor(f"b{tag}{i}", [P, 512], F32)).ap()
                   for i in range(6)]
            dbank = psb[:6] + [bt[0].bitcast(F32), bt[1].bitcast(F32)]
            return dict(wbg=wbg, wbu=wbu, wbd=wbd, psb=psb, dbank=dbank), bt

        with ExitStack() as p1:
            B1, bt1 = make_ffn_bufs(p1, "f1_")
            blk = p1.enter_context(nc.Block())
            E.set_block(blk)
            gate_phase({k: v for k, v in bar_p0.items()
                        if k not in ("c_cast",)})
            sub = {1: "ln", 2: "gateup"}.get(stop, "full")
            cast_chunks = [(m, k) for m in "qkvo" for k in range(8)]
            cast_skip = [4]
            def cast_hook():
                if cast_skip[0] > 0:
                    cast_skip[0] -= 1
                    return
                if not cast_chunks:
                    return
                m, k = cast_chunks.pop(0)
                w_pe = pe.n
                c_cast.bump(16)
                def gc(g, m=m, k=k, w_pe=w_pe):
                    g.wait_ge(pe.sem, w_pe)
                    g.dma_start(out=wnat[m].ap()[256 * k:256 * (k + 1), :],
                                in_=w_in[m][256 * k:256 * (k + 1), :]
                                ).then_inc(c_cast.sem, 16)
                E.on("gpsimd", gc)
            emit_ffn(B1, bt1, sub=sub, tile_hook=cast_hook)
            while cast_chunks:
                cast_hook()
            cast_all = c_cast.n
            bar_p1 = snap()
            E.flush()

        if stop <= 3:
            with nc.Block() as blk:
                E.set_block(blk)
                if dumpln:
                    w1 = mkwaits(bar_p1)
                    tgt = c_shst.bump(16)
                    def gd(g):
                        w1(g)
                        g.dma_start(out=dln, in_=xlnT).then_inc(c_shst.sem, 16)
                        g.wait_ge(c_shst.sem, tgt)
                    E.on("gpsimd", gd)
                if dumph:
                    w1 = mkwaits(bar_p1)
                    tgt = c_shst.bump(16)
                    def gd2(g):
                        w1(g)
                        g.dma_start(out=dh, in_=cbuf).then_inc(c_shst.sem, 16)
                        g.wait_ge(c_shst.sem, tgt)
                    E.on("gpsimd", gd2)
                emit_store(bar_p1)
                E.flush()
            return nc

        with ExitStack() as p2:
            wpb = [p2.enter_context(nc.sbuf_tensor(f"wpb{i}", [P, D], BF16)).ap()
                   for i in range(2)]
            wvb = [p2.enter_context(nc.sbuf_tensor(f"wvb{i}", [P, 512], BF16)).ap()
                   for i in range(4)]
            kstg = [p2.enter_context(nc.sbuf_tensor(f"kstg{i}", [P, 1024], BF16)).ap()
                    for i in range(2)]
            vstg = [p2.enter_context(nc.sbuf_tensor(f"vstg{i}", [P, 4096], BF16)).ap()
                    for i in range(2)]
            bt2 = [p2.enter_context(nc.psum_tensor(f"bt2_{i}", [P, 1024], BF16)).ap()
                   for i in range(2)]
            ps2 = [p2.enter_context(nc.psum_tensor(f"b2_{i}", [P, 512], F32)).ap()
                   for i in range(6)]
            vbank = ps2[:6] + [bt2[0].bitcast(F32), bt2[1].bitcast(F32)]
            blk = p2.enter_context(nc.Block())
            E.set_block(blk)
            gate_phase(bar_p1)

            emit_ln_all(bt2)
            ln2_done = dve.n
            ln2_act = act.n

            cast_waited = [False]

            def proj_oxT(wname, bname, dst_fn, first_wait=None):
                pe_h = []
                for oc in range(NH):
                    par = oc % 2
                    w_pe = pe.n if oc < 2 else pe_h[oc - 2]
                    wp_cnt = c_wp[par].bump(16)
                    need_cast = not cast_waited[0]
                    cast_waited[0] = True
                    def ldp(s, oc=oc, par=par, wname=wname, w_pe=w_pe,
                            need_cast=need_cast):
                        if need_cast:
                            s.wait_ge(c_cast.sem, cast_all)
                        s.wait_ge(pe.sem, w_pe)
                        s.dma_start(
                            out=wpb[par].rearrange("p (c o) -> p c o", o=P),
                            in_=wnat[wname].ap()[128 * oc:128 * (oc + 1), :],
                            transpose=True).then_inc(c_wp[par].sem, 16)
                    E.on("sync", ldp)
                    for th in range(2):
                        bb = ps2[2 * (oc % 2) + th]
                        fw = first_wait if (oc == 0 and th == 0) else None
                        w_dve = dve.n
                        pe_p = pe.bump()
                        def mmp(t, oc=oc, th=th, par=par, bb=bb, wp_cnt=wp_cnt,
                                fw=fw, w_dve=w_dve, bname=bname):
                            if fw is not None:
                                fw(t)
                            t.wait_ge(c_wp[par].sem, wp_cnt)
                            t.wait_ge(dve.sem, w_dve)
                            for c in range(NH):
                                t.matmul(bb, wpb[par][:, 128 * c:128 * (c + 1)],
                                         xlnT[:, c * T + 512 * th:
                                              c * T + 512 * (th + 1)],
                                         start=(c == 0), stop=False,
                                         skip_group_check=True)
                            t.matmul(bb, brow[bname][0:1, 128 * oc:128 * (oc + 1)],
                                     ones_bf[0:1, 0:512], start=False, stop=True,
                                     skip_group_check=True).then_inc(pe.sem, 1)
                        E.on("tensor", mmp)
                        dst_fn(oc, th, bb, pe_p)
                    pe_h.append(pe.n)

            kst_done = {}
            def k_dst(oc, th, bb, pe_p):
                par = oc % 2
                w_st = c_st2.n if oc >= 2 else None
                w_dve = dve.n
                dve_k = dve.bump()
                def evk(v, th=th, bb=bb, pe_p=pe_p, w_st=w_st, w_dve=w_dve,
                        par=par):
                    v.wait_ge(pe.sem, pe_p)
                    if w_st is not None:
                        v.wait_ge(c_st2.sem, w_st)
                    v.wait_ge(dve.sem, w_dve)
                    v.tensor_copy(kstg[par][:, 512 * th:512 * (th + 1)],
                                  bb).then_inc(dve.sem, 1)
                E.on("vector", evk)
                if th == 1:
                    kst_done[oc] = c_st2.bump(16)
                    def stk(g, oc=oc, par=par, dve_k=dve_k):
                        g.wait_ge(dve.sem, dve_k)
                        g.dma_start(
                            out=kT_loc.ap()[128 * oc:128 * (oc + 1), :],
                            in_=kstg[par]).then_inc(c_st2.sem, 16)
                    E.on("gpsimd", stk)

            def _kfw(t):
                t.wait_ge(dve.sem, ln2_done)
                t.wait_ge(act.sem, ln2_act)
            proj_oxT("k", "k", k_dst, first_wait=_kfw)

            st_k = c_st2.n
            cc["kt"].bump()
            def gagk(g, st_k=st_k):
                g.wait_ge(c_st2.sem, st_k)
                g.collective_compute(
                    "AllGather", ALU.bypass, replica_groups=[list(range(NCORE))],
                    ins=[kT_loc.ap()], outs=[kT_all.ap()]).then_inc(cc["kt"].sem, 1)
            E.on("gpsimd", gagk)

            vst_done = {}
            for og in range(4):
                pe_hv = []
                for c in range(NH):
                    w_pe = pe.n if c < 4 else pe_hv[c - 4]
                    wv_cnt = c_wv[c % 4].bump(16)
                    def ldv(s, og=og, c=c, w_pe=w_pe):
                        s.wait_ge(pe.sem, w_pe)
                        s.dma_start(
                            out=wvb[c % 4],
                            in_=wnat["v"].ap()[512 * og:512 * (og + 1),
                                               128 * c:128 * (c + 1)],
                            transpose=True).then_inc(c_wv[c % 4].sem, 16)
                    E.on("sync", ldv)
                    w_dve = dve.n if c == 0 else None
                    pe.bump()
                    def mmv(t, og=og, c=c, wv_cnt=wv_cnt, w_dve=w_dve):
                        if w_dve is not None:
                            t.wait_ge(dve.sem, w_dve)
                        t.wait_ge(c_wv[c % 4].sem, wv_cnt)
                        for tt in range(NT):
                            inst = t.matmul(
                                vbank[tt],
                                xlnT[:, c * T + 128 * tt: c * T + 128 * (tt + 1)],
                                wvb[c % 4],
                                start=(c == 0), stop=False, skip_group_check=True)
                            if tt == NT - 1:
                                inst.then_inc(pe.sem, 1)
                    E.on("tensor", mmv)
                    pe_hv.append(pe.n)
                pe_v = pe.bump()
                def mmvb(t, og=og):
                    for tt in range(NT):
                        inst = t.matmul(vbank[tt], ones_bf[0:1, 0:128],
                                        brow["v"][0:1, 512 * og:512 * (og + 1)],
                                        start=False, stop=True,
                                        skip_group_check=True)
                        if tt == NT - 1:
                            inst.then_inc(pe.sem, 1)
                E.on("tensor", mmvb)
                par = og % 2
                w_st = c_st2.n if og >= 2 else None
                for tt in range(NT):
                    w_dve = dve.n
                    dve_v2 = dve.bump()
                    def evv(v, tt=tt, pe_v=pe_v, w_st=(w_st if tt == 0 else None),
                            w_dve=w_dve, par=par):
                        v.wait_ge(pe.sem, pe_v)
                        if w_st is not None:
                            v.wait_ge(c_st2.sem, w_st)
                        v.wait_ge(dve.sem, w_dve)
                        v.tensor_copy(vstg[par][:, 512 * tt:512 * (tt + 1)],
                                      vbank[tt]).then_inc(dve.sem, 1)
                    E.on("vector", evv)
                vst_done[og] = c_st2.bump(16)
                def stv(g, og=og, par=par, dve_v2=dve_v2):
                    g.wait_ge(dve.sem, dve_v2)
                    g.dma_start(
                        out=v_loc.ap().rearrange("(tt p) d -> p tt d", p=P)
                            [:, :, 512 * og:512 * (og + 1)],
                        in_=vstg[par].rearrange("p (tt d) -> p tt d", d=512)
                    ).then_inc(c_st2.sem, 16)
                E.on("gpsimd", stv)

            st_v = c_st2.n
            cc["v"].bump()
            def gagv(g, st_v=st_v):
                g.wait_ge(c_st2.sem, st_v)
                g.collective_compute(
                    "AllGather", ALU.bypass, replica_groups=[list(range(NCORE))],
                    ins=[v_loc.ap()], outs=[v_all.ap()]).then_inc(cc["v"].sem, 1)
            E.on("gpsimd", gagv)

            def q_dst(oc, th, bb, pe_p):
                w_dve = dve.n
                dve.bump()
                def evq(v, oc=oc, th=th, bb=bb, pe_p=pe_p, w_dve=w_dve):
                    v.wait_ge(pe.sem, pe_p)
                    v.wait_ge(dve.sem, w_dve)
                    v.tensor_copy(
                        qT[:, oc * T + 512 * th: oc * T + 512 * (th + 1)],
                        bb).then_inc(dve.sem, 1)
                E.on("vector", evq)

            proj_oxT("q", "q", q_dst)
            bar_p2 = snap()
            E.flush()

        if stop == 4:
            with nc.Block() as blk:
                E.set_block(blk)
                emit_store(bar_p2)
                E.flush()
            return nc

        with ExitStack() as p3:
            ksb = [p3.enter_context(nc.sbuf_tensor(f"ksb{i}", [P, 2048], BF16)).ap()
                   for i in range(2)]
            vsb = [p3.enter_context(nc.sbuf_tensor(f"vsb{i}", [P, 16 * 129], BF16)).ap()
                   for i in range(2)]
            pT = [p3.enter_context(nc.sbuf_tensor(f"pT{i}", [P, T], BF16)).ap()
                  for i in range(4)]
            attn_sb = p3.enter_context(nc.sbuf_tensor("attn_sb", [P, T], BF16)).ap()
            rcp = p3.enter_context(nc.sbuf_tensor("rcp", [P, 8], F32)).ap()
            ps_s = [p3.enter_context(nc.psum_tensor(f"ps_s{i}", [P, T], F32)).ap()
                    for i in range(2)]
            ps_a = [p3.enter_context(nc.psum_tensor("ps_a0", [P, 387], F32)).ap(),
                    p3.enter_context(nc.psum_tensor("ps_a1", [P, 387], F32)).ap(),
                    p3.enter_context(nc.psum_tensor("ps_a2", [P, 258], F32)).ap()]
            ps_tr = p3.enter_context(nc.psum_tensor("ps_tr", [P, 512], BF16)).ap()
            blk = p3.enter_context(nc.Block())
            E.set_block(blk)
            gate_phase(bar_p2)

            def acc_slice(j):
                return ps_a[j // 3][:, 129 * (j % 3):129 * (j % 3) + 129]

            def acc_out(j):
                return ps_a[j // 3][:, 129 * (j % 3):129 * (j % 3) + 128]

            def acc_sum(j):
                return ps_a[j // 3][:, 129 * (j % 3) + 128:129 * (j % 3) + 129]

            kT4 = kT_all.ap().rearrange("(r p) (c f) -> p r c f", p=D, f=P)
            v3 = v_all.ap().rearrange("(j p) d -> p j d", p=P)

            pool_ones2 = pool.bump(2)
            def gones(g):
                for par in range(2):
                    g.memset(
                        vsb[par].rearrange("p (j f) -> p j f", f=129)[:, :, 128:129],
                        1.0).then_inc(pool.sem, 1)
            E.on("gpsimd", gones)

            sc_cnt = {}
            pv_cnt = {}
            exp_cnt = {}
            kload = {}
            vload = {}
            NB = NH * 4

            def emit_load(gb):
                h, b = gb // 4, gb % 4
                w_sc = sc_cnt.get(16 * (gb - 2) + 15)
                w_pv = pv_cnt.get(16 * (gb - 2) + 15)
                kload[gb] = c_k[gb % 2].bump(16)
                vload[gb] = c_v[gb % 2].bump(16)
                def lkv(s, gb=gb, h=h, b=b, w_sc=w_sc, w_pv=w_pv):
                    if gb < 2:
                        s.wait_ge(cc["kt"].sem, 1)
                        s.wait_ge(cc["v"].sem, 1)
                    if w_sc is not None:
                        s.wait_ge(pe.sem, w_sc)
                    s.dma_start(
                        out=ksb[gb % 2].rearrange("p (r c f) -> p r c f", r=2, f=P),
                        in_=kT4[128 * h:128 * (h + 1), 2 * b:2 * b + 2, :, :]
                    ).then_inc(c_k[gb % 2].sem, 16)
                    if w_pv is not None:
                        s.wait_ge(pe.sem, w_pv)
                    if gb == 0:
                        s.wait_ge(pool.sem, pool_ones2)
                    s.dma_start(
                        out=vsb[gb % 2].rearrange("p (j f) -> p j f", f=129)
                            [:, :, 0:128],
                        in_=v3[:, 16 * b:16 * (b + 1), 128 * h:128 * (h + 1)]
                    ).then_inc(c_v[gb % 2].sem, 16)
                E.on("sync", lkv)

            def emit_scores(gc):
                gb, kc = gc // 16, gc % 16
                h = gb // 4
                pp = gc % 2
                w_exp = exp_cnt.get(gc - 2)
                w_k = kload[gb]
                sc_cnt[gc] = pe.bump()
                def tsc(t, gb=gb, kc=kc, h=h, pp=pp, w_exp=w_exp, w_k=w_k):
                    t.wait_ge(c_k[gb % 2].sem, w_k)
                    if w_exp is not None:
                        t.wait_ge(act.sem, w_exp)
                    lhs = ksb[gb % 2][:, 128 * kc:128 * (kc + 1)]
                    t.matmul(ps_s[pp][:, 0:512], lhs, qT[:, h * T: h * T + 512],
                             start=True, stop=True, skip_group_check=True)
                    t.matmul(ps_s[pp][:, 512:1024], lhs,
                             qT[:, h * T + 512: h * T + 1024],
                             start=True, stop=True,
                             skip_group_check=True).then_inc(pe.sem, 1)
                E.on("tensor", tsc)

            def emit_exp(gc):
                pp = gc % 2
                p4_ = gc % 4
                w_sc = sc_cnt[gc]
                w_pv = pv_cnt.get(gc - 4)
                exp_cnt[gc] = act.bump()
                def sce(sc, pp=pp, p4_=p4_, w_sc=w_sc, w_pv=w_pv):
                    sc.wait_ge(pe.sem, w_sc)
                    if w_pv is not None:
                        sc.wait_ge(pe.sem, w_pv)
                    sc.activation(pT[p4_], ps_s[pp], AF.Exp,
                                  scale=SCALE).then_inc(act.sem, 1)
                E.on("scalar", sce)

            def emit_pv(gc, dve_prev_norm):
                gb, kc = gc // 16, gc % 16
                kin = gc % 64
                pp = gc % 2
                w_exp = exp_cnt[gc]
                w_v = vload[gb]
                pv_cnt[gc] = pe.bump()
                p4_ = gc % 4
                def tpv(t, gb=gb, kc=kc, kin=kin, p4_=p4_, w_exp=w_exp,
                        w_v=w_v, dve_prev_norm=dve_prev_norm):
                    t.wait_ge(act.sem, w_exp)
                    t.wait_ge(c_v[gb % 2].sem, w_v)
                    for j in range(8):
                        if kin == 0 and dve_prev_norm is not None and j % 3 == 0:
                            t.wait_ge(dve.sem, dve_prev_norm[min(j + 2, 7)])
                        inst = t.matmul(
                            acc_slice(j), pT[p4_][:, 128 * j:128 * (j + 1)],
                            vsb[gb % 2][:, 129 * kc:129 * kc + 129],
                            start=(kin == 0 and j % 3 == 0), stop=(kin == 63),
                            skip_group_check=True)
                        if j == 7:
                            inst.then_inc(pe.sem, 1)
                E.on("tensor", tpv)

            tr_state = {"pe_tr": None, "prev_ev": None, "dve_norm": None}

            def flash_head_end(h, pv_last):
                w_pe_tr = tr_state["pe_tr"]
                w_dve = dve.n
                rec_seq = [dve.bump() for _ in range(8)]
                for _ in range(8):
                    dve.bump()
                norm_seq = [rec_seq[7] + j + 1 for j in range(8)]
                def vnorm(v, h=h, pv_last=pv_last, w_pe_tr=w_pe_tr, w_dve=w_dve,
                          rec_seq=tuple(rec_seq)):
                    v.wait_ge(pe.sem, pv_last)
                    if w_pe_tr is not None:
                        v.wait_ge(pe.sem, w_pe_tr)
                    v.wait_ge(dve.sem, w_dve)
                    for j in range(8):
                        if j > 0:
                            v.wait_ge(dve.sem, rec_seq[j - 1])
                        v.reciprocal(rcp[:, j:j + 1], acc_sum(j)).then_inc(dve.sem, 1)
                    v.wait_ge(dve.sem, rec_seq[7])
                    for j in range(8):
                        v.tensor_scalar(
                            attn_sb[:, 128 * j:128 * (j + 1)], acc_out(j),
                            rcp[:, j:j + 1], None, ALU.mult).then_inc(dve.sem, 1)
                E.on("vector", vnorm)
                tr_state["dve_norm"] = norm_seq
                dn = dve.n
                for g4 in range(2):
                    w_prev = tr_state["prev_ev"]
                    pe_tr = pe.bump()
                    def ttr(t, g4=g4, dn=dn, w_prev=w_prev):
                        t.wait_ge(dve.sem, dn)
                        if w_prev is not None:
                            t.wait_ge(dve.sem, w_prev)
                        for i in range(4):
                            j = 4 * g4 + i
                            inst = t.transpose(ps_tr[:, 128 * i:128 * (i + 1)],
                                               attn_sb[:, 128 * j:128 * (j + 1)],
                                               ident)
                            if i == 3:
                                inst.then_inc(pe.sem, 1)
                    E.on("tensor", ttr)
                    w_dve5 = dve.n
                    tr_state["prev_ev"] = dve.bump()
                    def evt(v, h=h, g4=g4, pe_tr=pe_tr, w_dve5=w_dve5):
                        v.wait_ge(pe.sem, pe_tr)
                        v.wait_ge(dve.sem, w_dve5)
                        v.tensor_copy(
                            attnT[:, h * T + 512 * g4: h * T + 512 * (g4 + 1)],
                            ps_tr).then_inc(dve.sem, 1)
                    E.on("vector", evt)
                tr_state["pe_tr"] = pe.n

            emit_load(0)
            emit_load(1)
            for h in range(NH):
                base = 64 * h
                for kin in range(64):
                    gc = base + kin
                    emit_scores(gc)
                    if gc > 0:
                        emit_exp(gc - 1)
                        emit_pv(gc - 1, tr_state["dve_norm"]
                                if (gc - 1) % 64 == 0 else None)
                        if (gc - 1) % 64 == 63:
                            flash_head_end(h - 1, pv_cnt[gc - 1])
                        if (gc - 1) % 16 == 15 and ((gc - 1) // 16) + 2 < NB:
                            emit_load((gc - 1) // 16 + 2)
            emit_exp(64 * NH - 1)
            emit_pv(64 * NH - 1, None)
            flash_head_end(NH - 1, pv_cnt[64 * NH - 1])

            bar_p3 = snap()
            E.flush()

        if stop == 5:
            with nc.Block() as blk:
                E.set_block(blk)
                emit_store(bar_p3)
                E.flush()
            return nc

        with ExitStack() as p4:
            wob = [p4.enter_context(nc.sbuf_tensor(f"wob{i}", [P, 512], BF16)).ap()
                   for i in range(4)]
            B4 = [p4.enter_context(nc.psum_tensor(f"b4_{i}", [P, 512], F32)).ap()
                  for i in range(8)]
            blk = p4.enter_context(nc.Block())
            E.set_block(blk)
            gate_phase(bar_p3)

            for og in range(4):
                pe_ho = []
                for c in range(NH):
                    w_pe = pe.n if c < 4 else pe_ho[c - 4]
                    wo_cnt = c_wv[c % 4].bump(16)
                    def ldo(s, og=og, c=c, w_pe=w_pe):
                        s.wait_ge(pe.sem, w_pe)
                        s.dma_start(
                            out=wob[c % 4],
                            in_=wnat["o"].ap()[512 * og:512 * (og + 1),
                                               128 * c:128 * (c + 1)],
                            transpose=True).then_inc(c_wv[c % 4].sem, 16)
                    E.on("sync", ldo)
                    w_dve = dve.n if c == 0 else None
                    pe.bump()
                    def mmo(t, og=og, c=c, wo_cnt=wo_cnt, w_dve=w_dve):
                        if w_dve is not None:
                            t.wait_ge(dve.sem, w_dve)
                        t.wait_ge(c_wv[c % 4].sem, wo_cnt)
                        for tt in range(NT):
                            inst = t.matmul(
                                B4[tt],
                                attnT[:, c * T + 128 * tt: c * T + 128 * (tt + 1)],
                                wob[c % 4],
                                start=(c == 0), stop=False, skip_group_check=True)
                            if tt == NT - 1:
                                inst.then_inc(pe.sem, 1)
                    E.on("tensor", mmo)
                    pe_ho.append(pe.n)
                pe_o = pe.bump()
                def mmob(t, og=og):
                    for tt in range(NT):
                        inst = t.matmul(B4[tt], ones_bf[0:1, 0:128],
                                        brow["o"][0:1, 512 * og:512 * (og + 1)],
                                        start=False, stop=True,
                                        skip_group_check=True)
                        if tt == NT - 1:
                            inst.then_inc(pe.sem, 1)
                E.on("tensor", mmob)
                w_dve6 = dve.n
                vro_seq = [dve.bump() for _ in range(NT)]
                def vro(v, og=og, pe_o=pe_o, w_dve6=w_dve6, vro_seq=tuple(vro_seq)):
                    v.wait_ge(pe.sem, pe_o)
                    v.wait_ge(dve.sem, w_dve6)
                    for tt in range(NT):
                        if tt > 0:
                            v.wait_ge(dve.sem, vro_seq[tt - 1])
                        v.scalar_tensor_tensor(
                            xres3[:, tt, 512 * og:512 * (og + 1)],
                            B4[tt], 1.0,
                            xres3[:, tt, 512 * og:512 * (og + 1)],
                            ALU.mult, ALU.add).then_inc(dve.sem, 1)
                E.on("vector", vro)
            bar_p4 = snap()
            E.flush()

        if stop == 6:
            with nc.Block() as blk:
                E.set_block(blk)
                emit_store(bar_p4)
                E.flush()
            return nc

        with ExitStack() as p5:
            B5, bt5 = make_ffn_bufs(p5, "f5_")
            blk = p5.enter_context(nc.Block())
            E.set_block(blk)
            gate_phase(bar_p4)
            emit_ffn(B5, bt5)
            bar_p5 = snap()
            emit_store(bar_p5)
            E.flush()

    return nc


_NC_CACHE = {}


def _get_nc(stop_stage="ffn2"):
    if stop_stage not in _NC_CACHE:
        _NC_CACHE[stop_stage] = build_nc(stop_stage)
    return _NC_CACHE[stop_stage]


def kernel(L0_attn_hidden_states, L0_attn_wq, L0_attn_bq, L0_attn_wk, L0_attn_bk,
           L0_attn_wv, L0_attn_bv, L0_attn_wo, L0_attn_bo, L0_ffn_lut,
           L0_ffn_gate_idx, L0_ffn_up_idx, L0_ffn_down_idx, stop_stage="ffn2"):
    x = np.ascontiguousarray(np.asarray(L0_attn_hidden_states, dtype=np.float32)
                             .reshape(SEQ, D))
    arrs = dict(
        wq=np.ascontiguousarray(np.asarray(L0_attn_wq, dtype=np.float32)),
        wk=np.ascontiguousarray(np.asarray(L0_attn_wk, dtype=np.float32)),
        wv=np.ascontiguousarray(np.asarray(L0_attn_wv, dtype=np.float32)),
        wo=np.ascontiguousarray(np.asarray(L0_attn_wo, dtype=np.float32)),
        bq=np.ascontiguousarray(np.asarray(L0_attn_bq, dtype=np.float32)),
        bk=np.ascontiguousarray(np.asarray(L0_attn_bk, dtype=np.float32)),
        bv=np.ascontiguousarray(np.asarray(L0_attn_bv, dtype=np.float32)),
        bo=np.ascontiguousarray(np.asarray(L0_attn_bo, dtype=np.float32)),
        lut=np.ascontiguousarray(np.asarray(L0_ffn_lut, dtype=np.float32)),
    )
    gidx = np.asarray(L0_ffn_gate_idx, dtype=np.int32)
    uidx = np.asarray(L0_ffn_up_idx, dtype=np.int32)
    didx = np.asarray(L0_ffn_down_idx, dtype=np.int32)

    nc = _get_nc(stop_stage)
    in_maps = []
    for r in range(NCORE):
        m = dict(arrs)
        m["x"] = np.ascontiguousarray(x[T * r:T * (r + 1)])
        m["gidx"] = np.ascontiguousarray(gidx[:, GSH * r:GSH * (r + 1)])
        m["uidx"] = np.ascontiguousarray(uidx[:, GSH * r:GSH * (r + 1)])
        m["didx"] = np.ascontiguousarray(didx[:, DSH * r:DSH * (r + 1)])
        in_maps.append(m)
    res = run_bass_kernel_spmd(nc, in_maps, core_ids=list(range(NCORE)))
    out = np.concatenate([res.results[r]["y"] for r in range(NCORE)], axis=0)
    return out.reshape(1, SEQ, D)


# version 21 (final)
# speedup vs baseline: 1.1081x; 1.1081x over previous
from contextlib import ExitStack

import numpy as np
import concourse.bass as bass
import concourse.mybir as mybir
from concourse.bass_utils import run_bass_kernel_spmd

F32 = mybir.dt.float32
BF16 = mybir.dt.bfloat16
I32 = mybir.dt.int32
I16 = mybir.dt.int16
AF = mybir.ActivationFunctionType
ALU = mybir.AluOpType
AXX = mybir.AxisListType.X

P = 128
D = 2048
SEQ = 8192
NCORE = 8
T = 1024
NT = T // P
NH = D // P
INTER = 5504
NI = INTER // P
GSH = D // NCORE
DSH = INTER // NCORE
HALVES = [(0, 22), (22 * 128, 21)]
SCALE = float(1.0 / np.sqrt(128.0))
EPS = 1e-5
ENGS = ["sync", "scalar", "vector", "tensor", "gpsimd"]


class Ctr:
    def __init__(self, stack, nc, name, registry):
        self.sem = stack.enter_context(nc.semaphore(name))
        self.n = 0
        registry[name] = self

    def bump(self, k=1):
        self.n += k
        return self.n


class Emitter:
    def __init__(self):
        self.q = {n: [] for n in ENGS}
        self.blk = None

    def set_block(self, blk):
        self.blk = blk

    def on(self, name, fn):
        self.q[name].append(fn)

    def flush(self):
        for n in ENGS:
            fns = self.q[n]
            if fns:
                getattr(self.blk, n)(lambda e, fns=tuple(fns): [f(e) for f in fns])
                self.q[n] = []


def build_nc(stop_stage="ffn2"):
    STAGES = ["p0", "ln1", "gateup", "ffn1", "proj", "flash", "oproj", "ffn2"]
    _st = {"dumpw": "p0", "dumpln": "ln1", "dumph": "gateup"}.get(stop_stage, stop_stage)
    stop = STAGES.index(_st)
    dumpw = stop_stage == "dumpw"
    dumpln = stop_stage == "dumpln"
    dumph = stop_stage == "dumph"

    nc = bass.Bass("TRN2", target_bir_lowering=False)

    x_in = nc.dram_tensor("x", [T, D], F32, kind="ExternalInput").ap()
    w_in = {m: nc.dram_tensor("w" + m, [D, D], F32, kind="ExternalInput").ap()
            for m in "qkvo"}
    b_in = {m: nc.dram_tensor("b" + m, [D], F32, kind="ExternalInput").ap()
            for m in "qkvo"}
    lut_in = nc.dram_tensor("lut", [16], F32, kind="ExternalInput").ap()
    gidx_in = nc.dram_tensor("gidx", [INTER, GSH], I32, kind="ExternalInput").ap()
    uidx_in = nc.dram_tensor("uidx", [INTER, GSH], I32, kind="ExternalInput").ap()
    didx_in = nc.dram_tensor("didx", [D, DSH], I32, kind="ExternalInput").ap()
    y_out = nc.dram_tensor("y", [T, D], F32, kind="ExternalOutput").ap()
    if dumpln:
        dln = nc.dram_tensor("dln", [P, NH * T], BF16, kind="ExternalOutput").ap()
    if dumph:
        dh = nc.dram_tensor("dh", [P, 22 * T], BF16, kind="ExternalOutput").ap()
    if dumpw:
        dw_g = nc.dram_tensor("dw_g", [D, INTER], BF16, kind="ExternalOutput").ap()
        dw_u = nc.dram_tensor("dw_u", [D, INTER], BF16, kind="ExternalOutput").ap()
        dw_d = nc.dram_tensor("dw_d", [INTER, D], BF16, kind="ExternalOutput").ap()
        dw_q = nc.dram_tensor("dw_q", [D, D], BF16, kind="ExternalOutput").ap()

    wg_sh = nc.dram_tensor("wg_sh", [GSH, INTER], BF16)
    wu_sh = nc.dram_tensor("wu_sh", [GSH, INTER], BF16)
    wd_sh = nc.dram_tensor("wd_sh", [DSH, D], BF16)
    wgT = nc.dram_tensor("wgT", [D, INTER], BF16, addr_space="Shared")
    wuT = nc.dram_tensor("wuT", [D, INTER], BF16, addr_space="Shared")
    wdT = nc.dram_tensor("wdT", [INTER, D], BF16, addr_space="Shared")
    wnat = {m: nc.dram_tensor("wnat" + m, [D, D], BF16) for m in "qkvo"}
    kT_loc = nc.dram_tensor("kT_loc", [D, T], BF16)
    v_loc = nc.dram_tensor("v_loc", [T, D], BF16)
    kT_all = nc.dram_tensor("kT_all", [NCORE * D, T], BF16, addr_space="Shared")
    v_all = nc.dram_tensor("v_all", [SEQ, D], BF16, addr_space="Shared")

    E = Emitter()
    st = ExitStack()
    with st:
        R = {}
        pe = Ctr(st, nc, "pe", R)
        act = Ctr(st, nc, "act", R)
        dve = Ctr(st, nc, "dve", R)
        pool = Ctr(st, nc, "pool", R)
        cc = {k: Ctr(st, nc, "cc_" + k, R) for k in ["wg", "wu", "wd", "kt", "v"]}
        c_idx = Ctr(st, nc, "c_idx", R)
        c_sml = Ctr(st, nc, "c_sml", R)
        c_cast = Ctr(st, nc, "c_cast", R)
        c_x = Ctr(st, nc, "c_x", R)
        c_shst = Ctr(st, nc, "c_shst", R)
        c_wg = [Ctr(st, nc, f"c_wg{i}", R) for i in range(2)]
        c_wu = [Ctr(st, nc, f"c_wu{i}", R) for i in range(2)]
        c_wd = [Ctr(st, nc, f"c_wd{i}", R) for i in range(4)]
        c_wp = [Ctr(st, nc, f"c_wp{i}", R) for i in range(2)]
        c_wv = [Ctr(st, nc, f"c_wv{i}", R) for i in range(4)]
        c_st2 = Ctr(st, nc, "c_st2", R)
        c_k = [Ctr(st, nc, f"c_k{i}", R) for i in range(2)]
        c_v = [Ctr(st, nc, f"c_v{i}", R) for i in range(2)]

        def snap():
            return {k: c.n for k, c in R.items()}

        def mkwaits(bar):
            items = [(R[k].sem, v) for k, v in bar.items()
                     if v > 0 and not k.startswith("cc_")]
            def w(e):
                for sem, v in items:
                    e.wait_ge(sem, v)
            return w

        def gate_phase(bar):
            w = mkwaits(bar)
            for name in ENGS:
                E.on(name, w)

        xres = nc.alloc_sbuf_tensor("xres", [P, NT * D], F32).ap()
        sqscr = nc.alloc_sbuf_tensor("sqscr", [P, D], F32).ap()
        xlntok = nc.alloc_sbuf_tensor("xlntok", [P, D], BF16).ap()
        ident = nc.alloc_sbuf_tensor("ident", [P, P], BF16).ap()
        ones_bf = nc.alloc_sbuf_tensor("ones_bf", [1, 512], BF16).ap()
        onesf = nc.alloc_sbuf_tensor("onesf", [1, P], F32).ap()
        lutrow = nc.alloc_sbuf_tensor("lutrow", [1, 16], F32).ap()
        lutbc = nc.alloc_sbuf_tensor("lutbc", [P, 16], F32).ap()
        brow = {m: nc.alloc_sbuf_tensor("brow" + m, [1, D], BF16).ap() for m in "qkvo"}
        sums = [nc.alloc_sbuf_tensor(f"sums{i}", [P, 1], F32).ap() for i in range(2)]
        sumsq = [nc.alloc_sbuf_tensor(f"sumsq{i}", [P, 1], F32).ap() for i in range(2)]
        mu = [nc.alloc_sbuf_tensor(f"mu{i}", [P, 1], F32).ap() for i in range(2)]
        musq = [nc.alloc_sbuf_tensor(f"musq{i}", [P, 1], F32).ap() for i in range(2)]
        var = [nc.alloc_sbuf_tensor(f"var{i}", [P, 1], F32).ap() for i in range(2)]
        rstd = [nc.alloc_sbuf_tensor(f"rstd{i}", [P, 1], F32).ap() for i in range(2)]
        nmu = [nc.alloc_sbuf_tensor(f"nmu{i}", [P, 1], F32).ap() for i in range(2)]
        xlntok2 = [xlntok, xlntok]

        x_view = x_in.rearrange("(j p) h -> p j h", p=P)
        y_view = y_out.rearrange("(j p) h -> p j h", p=P)
        xres3 = xres.rearrange("p (j h) -> p j h", h=D)

        def emit_store(bar):
            w = mkwaits(bar)
            yc = c_x.bump(16)
            def sfin(s):
                w(s)
                s.dma_start(out=y_view, in_=xres3).then_inc(c_x.sem, 16)
                s.wait_ge(c_x.sem, yc)
            E.on("sync", sfin)

        with ExitStack() as p0:
            idxi = p0.enter_context(nc.sbuf_tensor("idxi", [P, 11008], I16)).ap()
            acc0 = p0.enter_context(nc.sbuf_tensor("acc0", [P, 11008], BF16)).ap()
            acc1 = p0.enter_context(nc.sbuf_tensor("acc1", [P, 11008], BF16)).ap()
            mmsk = p0.enter_context(nc.sbuf_tensor("mmsk", [P, 5504], BF16)).ap()
            rowbuf = p0.enter_context(nc.sbuf_tensor("rowbuf", [P, INTER], BF16)).ap()
            mmsk_p = sqscr.bitcast(BF16)[:, 0:2752]
            ps_lut = p0.enter_context(nc.psum_tensor("ps_lut", [P, 16], F32)).ap()
            ps_t = [p0.enter_context(nc.psum_tensor(f"ps_t{i}", [P, P], BF16)).ap()
                    for i in range(2)]
            blk = p0.enter_context(nc.Block())
            E.set_block(blk)

            idx_g_cnt = c_idx.bump(16)
            sml_cnt = c_sml.bump(16 * 5)
            pool_ones = pool.bump(2)
            pool_ident = pool.bump(2)
            def g0(g):
                g.dma_start(out=idxi.rearrange("p (a h) -> p a h", h=GSH),
                            in_=gidx_in.rearrange("(a p) h -> p a h", p=P)
                            ).then_inc(c_idx.sem, 16)
                g.dma_start(out=lutrow[0:1, :],
                            in_=lut_in.unsqueeze(0)).then_inc(c_sml.sem, 16)
                for m in "qkvo":
                    g.dma_start(out=brow[m][0:1, :],
                                in_=b_in[m].unsqueeze(0)).then_inc(c_sml.sem, 16)
                g.memset(ones_bf[0:1, :], 1.0).then_inc(pool.sem, 1)
                g.memset(onesf[0:1, :], 1.0).then_inc(pool.sem, 1)
                g.memset(ident[:, :], 1.0).then_inc(pool.sem, 1)
                g.wait_ge(pool.sem, pool_ones + 1)
                g.affine_select(ident[:, :], ident[:, :], [[-1, P]],
                                ALU.is_equal, 0.0, base=0,
                                channel_multiplier=1).then_inc(pool.sem, 1)
            E.on("gpsimd", g0)

            x_cnt = c_x.bump(16)
            E.on("sync", lambda s: s.dma_start(out=xres3, in_=x_view)
                 .then_inc(c_x.sem, 16))

            pe_lut = pe.bump()
            def t0(t):
                t.wait_ge(c_sml.sem, sml_cnt)
                t.wait_ge(pool.sem, pool_ones)
                t.matmul(ps_lut, onesf[0:1, :], lutrow[0:1, :], start=True,
                         stop=True, skip_group_check=True).then_inc(pe.sem, 1)
            E.on("tensor", t0)

            dve_lutbc = dve.bump()
            def v0(v):
                v.wait_ge(pe.sem, pe_lut)
                v.tensor_copy(lutbc, ps_lut).then_inc(dve.sem, 1)
            E.on("vector", v0)

            def materialize(accb, idx_cnt, pe_wait=None):
                seq = []
                idxfree = 0
                for tt in range(16):
                    for hh in range(2):
                        sl = slice(5504 * hh, 5504 * (hh + 1))
                        seq.append(("mask", sl, tt, dve.n))
                        dve.bump()
                        if tt > 0:
                            seq.append(("add", sl, tt, dve.n))
                            dve.bump()
                    if tt == 15:
                        idxfree = dve.n
                done = dve.n
                def vm(v, seq=tuple(seq)):
                    v.wait_ge(c_idx.sem, idx_cnt)
                    if pe_wait is not None:
                        v.wait_ge(pe.sem, pe_wait)
                    for kind, sl, tt, wv in seq:
                        v.wait_ge(dve.sem, wv)
                        if kind == "mask":
                            dst = accb[:, sl] if tt == 0 else mmsk
                            v.tensor_scalar(dst, idxi[:, sl], float(tt),
                                            lutbc[:, tt:tt + 1], ALU.is_equal,
                                            ALU.mult).then_inc(dve.sem, 1)
                        else:
                            v.tensor_tensor(accb[:, sl], accb[:, sl], mmsk,
                                            ALU.add).then_inc(dve.sem, 1)
                E.on("vector", vm)
                return (idxfree, 0), (done, 0)

            def emit_transposes(accb, pieces, natiles, piece_stride, done_dve,
                                sh_dram, out_dram, cname, ncols):
                done_d, done_p = done_dve
                k = 0
                for (off, cnt) in pieces:
                    for a in range(natiles):
                        par = k % 2
                        tr_w_act = act.n if k < 2 else act.n - 1
                        tr_tgt = pe.bump()
                        def tt_(t, a=a, off=off, cnt=cnt, par=par, k=k,
                                tr_w_act=tr_w_act):
                            if k == 0:
                                t.wait_ge(dve.sem, done_d)
                                t.wait_ge(pool.sem, max(done_p, pool_ident))
                            if tr_w_act > 0:
                                t.wait_ge(act.sem, tr_w_act)
                            t.transpose(ps_t[par][0:cnt, :],
                                        accb[:, a * piece_stride + off:
                                             a * piece_stride + off + cnt],
                                        ident).then_inc(pe.sem, 1)
                        E.on("tensor", tt_)
                        shst_w = c_shst.n if a == 0 else None
                        act.bump()
                        def ev(sc, a=a, cnt=cnt, par=par, tr_tgt=tr_tgt,
                               shst_w=shst_w):
                            sc.wait_ge(pe.sem, tr_tgt)
                            if shst_w is not None:
                                sc.wait_ge(c_shst.sem, shst_w)
                            sc.copy(rowbuf[0:cnt, 128 * a:128 * a + 128],
                                    ps_t[par][0:cnt, :]).then_inc(act.sem, 1)
                        E.on("scalar", ev)
                        k += 1
                    act_w = act.n
                    c_shst.bump(16)
                    def gst(g, off=off, cnt=cnt, act_w=act_w):
                        g.wait_ge(act.sem, act_w)
                        g.dma_start(out=sh_dram.ap()[off:off + cnt, :],
                                    in_=rowbuf[0:cnt, 0:ncols]
                                    ).then_inc(c_shst.sem, 16)
                    E.on("gpsimd", gst)
                shst_all = c_shst.n
                cc[cname].bump()
                def gag(g, shst_all=shst_all):
                    g.wait_ge(c_shst.sem, shst_all)
                    g.collective_compute(
                        "AllGather", ALU.bypass,
                        replica_groups=[list(range(NCORE))],
                        ins=[sh_dram.ap()],
                        outs=[out_dram.ap()]).then_inc(cc[cname].sem, 1)
                E.on("gpsimd", gag)

            gfree, gdone = materialize(acc0, idx_g_cnt)

            idx_u_cnt = c_idx.bump(16)
            def g1(g, gfree=gfree):
                g.wait_ge(dve.sem, gfree[0])
                if gfree[1] > 0:
                    g.wait_ge(pool.sem, gfree[1])
                g.dma_start(out=idxi.rearrange("p (a h) -> p a h", h=GSH),
                            in_=uidx_in.rearrange("(a p) h -> p a h", p=P)
                            ).then_inc(c_idx.sem, 16)
            E.on("gpsimd", g1)

            ufree, udone = materialize(acc1, idx_u_cnt)
            emit_transposes(acc0, [(0, 128), (128, 128)], NI, GSH, gdone,
                            wg_sh, wgT, "wg", INTER)
            pe_gate_tr = pe.n

            idx_d_cnt = c_idx.bump(16)
            def g2(g, ufree=ufree):
                g.wait_ge(dve.sem, ufree[0])
                if ufree[1] > 0:
                    g.wait_ge(pool.sem, ufree[1])
                g.dma_start(out=idxi.rearrange("p (a i) -> p a i", i=DSH),
                            in_=didx_in.rearrange("(a p) i -> p a i", p=P)
                            ).then_inc(c_idx.sem, 16)
            E.on("gpsimd", g2)

            dfree, ddone = materialize(acc0, idx_d_cnt, pe_wait=pe_gate_tr)
            emit_transposes(acc1, [(0, 128), (128, 128)], NI, GSH, udone,
                            wu_sh, wuT, "wu", INTER)
            dpieces = [(128 * j, 128) for j in range(5)] + [(640, 48)]
            emit_transposes(acc0, dpieces, NH, DSH, ddone,
                            wd_sh, wdT, "wd", D)

            bar_p0 = snap()
            E.flush()

        if stop == 0:
            with nc.Block() as blk:
                E.set_block(blk)
                if dumpw:
                    w0 = mkwaits(bar_p0)
                    tgt = c_shst.bump(64)
                    def gdump(g):
                        w0(g)
                        for ccn in ["wg", "wu", "wd"]:
                            g.wait_ge(cc[ccn].sem, 1)
                        g.dma_start(out=dw_g, in_=wgT.ap()).then_inc(c_shst.sem, 16)
                        g.dma_start(out=dw_u, in_=wuT.ap()).then_inc(c_shst.sem, 16)
                        g.dma_start(out=dw_d, in_=wdT.ap()).then_inc(c_shst.sem, 16)
                        g.wait_ge(c_shst.sem, tgt)
                    E.on("gpsimd", gdump)
                emit_store(bar_p0)
                E.flush()
            return nc

        xlnT = nc.alloc_sbuf_tensor("xlnT", [P, NH * T], BF16).ap()
        attnT = xlnT
        cbuf = nc.alloc_sbuf_tensor("cbuf", [P, 22 * T], BF16).ap()
        hact = cbuf
        qT = cbuf

        ln_prev_aexp = {}
        ln_prev_varread = {}
        ln_prev_cp = {}
        ln_var = {}

        def emit_ln_stats(j, prev_asq=None):
            pr = j % 2
            w_varread = ln_prev_varread.get(j - 2)
            a_sq = act.bump()
            def ssq(sc, j=j, pr=pr, w_varread=w_varread, prev_asq=prev_asq):
                sc.wait_ge(c_x.sem, x_cnt)
                if prev_asq is not None:
                    sc.wait_ge(act.sem, prev_asq)
                if w_varread is not None:
                    sc.wait_ge(dve.sem, w_varread)
                sc.activation(sqscr, xres3[:, j, :], AF.Square,
                              accum_out=sumsq[pr]).then_inc(act.sem, 1)
            E.on("scalar", ssq)

            w0 = dve.n
            s_sum = dve.bump()
            s_mu = dve.bump()
            s_musq = dve.bump()
            s_var = dve.bump()
            s_sub = dve.bump()
            s_eps = dve.bump()
            w_aexp = ln_prev_aexp.get(j - 2)
            def vln(v, j=j, pr=pr, w0=w0, w_aexp=w_aexp, a_sq=a_sq):
                v.wait_ge(c_x.sem, x_cnt)
                v.wait_ge(dve.sem, w0)
                if w_aexp is not None:
                    v.wait_ge(act.sem, w_aexp)
                v.reduce_sum(sums[pr], xres3[:, j, :], axis=AXX).then_inc(dve.sem, 1)
                v.wait_ge(dve.sem, s_sum)
                v.tensor_scalar_mul(mu[pr], sums[pr], 1.0 / D).then_inc(dve.sem, 1)
                v.wait_ge(dve.sem, s_mu)
                v.scalar_tensor_tensor(musq[pr], mu[pr], 1.0, mu[pr], ALU.mult,
                                       ALU.mult).then_inc(dve.sem, 1)
                v.wait_ge(dve.sem, s_musq)
                v.wait_ge(act.sem, a_sq)
                v.tensor_scalar_mul(var[pr], sumsq[pr], 1.0 / D).then_inc(dve.sem, 1)
                v.wait_ge(dve.sem, s_var)
                v.tensor_sub(var[pr], var[pr], musq[pr]).then_inc(dve.sem, 1)
                v.wait_ge(dve.sem, s_sub)
                v.tensor_scalar_add(var[pr], var[pr], EPS).then_inc(dve.sem, 1)
            E.on("vector", vln)
            ln_prev_varread[j] = s_var
            ln_var[j] = s_eps
            return a_sq

        def emit_ln_norm(j):
            pr = j % 2
            dve_var = ln_var[j]
            a_ln = act.bump()
            a_exp = act.bump()
            ln_prev_aexp[j] = a_exp
            def scr(sc, pr=pr, dve_var=dve_var, a_ln=a_ln):
                sc.wait_ge(dve.sem, dve_var)
                sc.activation(sums[pr], var[pr], AF.Ln).then_inc(act.sem, 1)
                sc.wait_ge(act.sem, a_ln)
                sc.activation(rstd[pr], sums[pr], AF.Exp,
                              scale=-0.5).then_inc(act.sem, 1)
            E.on("scalar", scr)
            w_pe = pe.n
            s_nmu = dve.bump()
            s_norm = dve.bump()
            def vnm(v, j=j, pr=pr, a_exp=a_exp, w_pe=w_pe, s_nmu=s_nmu):
                v.wait_ge(act.sem, a_exp)
                v.scalar_tensor_tensor(nmu[pr], mu[pr], -1.0, rstd[pr], ALU.mult,
                                       ALU.mult).then_inc(dve.sem, 1)
                v.wait_ge(dve.sem, s_nmu)
                v.wait_ge(pe.sem, w_pe)
                v.tensor_scalar(xlntok2[pr], xres3[:, j, :], rstd[pr], nmu[pr],
                                ALU.mult, ALU.add).then_inc(dve.sem, 1)
            E.on("vector", vnm)
            return s_norm

        def emit_ln_transposes(j, bt, dve_norm):
            pr = j % 2
            cp_act = []
            for g4 in range(4):
                par = g4 % 2
                w_bt = ln_prev_cp.get(j - 1) if g4 < 2 else cp_act[g4 - 2]
                tr_tgt = pe.bump()
                def ttr(t, g4=g4, pr=pr, par=par, w_bt=w_bt, dve_norm=dve_norm):
                    t.wait_ge(dve.sem, dve_norm)
                    if w_bt is not None:
                        t.wait_ge(act.sem, w_bt)
                    for i in range(4):
                        c = g4 * 4 + i
                        inst = t.transpose(bt[par][:, 128 * i:128 * (i + 1)],
                                           xlntok2[pr][:, 128 * c:128 * (c + 1)],
                                           ident)
                        if i == 3:
                            inst.then_inc(pe.sem, 1)
                E.on("tensor", ttr)
                cp_act.append(act.bump())
                def evr(sc, j=j, g4=g4, par=par, tr_tgt=tr_tgt):
                    sc.wait_ge(pe.sem, tr_tgt)
                    sc.copy(
                        xlnT.rearrange("p (c t) -> p c t", t=T)
                            [:, 4 * g4:4 * g4 + 4, 128 * j:128 * (j + 1)],
                        bt[par][:, 0:512].rearrange("p (c t) -> p c t", t=P)
                    ).then_inc(act.sem, 1)
                E.on("scalar", evr)
            ln_prev_cp[j] = cp_act[-1]

        def emit_ln_all(bt):
            prev_asq = None
            for j in range(NT):
                prev_asq = emit_ln_stats(j, prev_asq)
                if j >= 1:
                    dn = emit_ln_norm(j - 1)
                    emit_ln_transposes(j - 1, bt, dn)
            dn = emit_ln_norm(NT - 1)
            emit_ln_transposes(NT - 1, bt, dn)

        def emit_ffn(B, bt, sub="full", tile_hook=None):
            emit_ln_all(bt)
            ln_done = dve.n
            if sub == "ln":
                return

            wgt3 = wgT.ap().rearrange("(c p) i -> p c i", p=P)
            wut3 = wuT.ap().rearrange("(c p) i -> p c i", p=P)
            wbg = B["wbg"]; wbu = B["wbu"]; wbd = B["wbd"]
            psb = B["psb"]

            for (io, na) in HALVES:
                pe_hist = []
                for a in range(na):
                    par = a % 2
                    w_pe = pe.n if a < 2 else pe_hist[a - 2]
                    wg_cnt = c_wg[par].bump(16)
                    wu_cnt = c_wu[par].bump(16)
                    def ldw(s, a=a, par=par, io=io, w_pe=w_pe):
                        if a == 0 and io == 0:
                            s.wait_ge(cc["wg"].sem, 1)
                            s.wait_ge(cc["wu"].sem, 1)
                        s.wait_ge(pe.sem, w_pe)
                        s.dma_start(
                            out=wbg[par].rearrange("p (c i) -> p c i", i=P),
                            in_=wgt3[:, :, io + 128 * a: io + 128 * (a + 1)]
                        ).then_inc(c_wg[par].sem, 16)
                        s.dma_start(
                            out=wbu[par].rearrange("p (c i) -> p c i", i=P),
                            in_=wut3[:, :, io + 128 * a: io + 128 * (a + 1)]
                        ).then_inc(c_wu[par].sem, 16)
                    E.on("sync", ldw)

                    for th in range(2):
                        w_dve_first = dve.n if (a == 0 and th == 0) else None
                        w_act = act.n
                        pe_gate = pe.bump()
                        def mmg(t, a=a, th=th, par=par, wg_cnt=wg_cnt,
                                w_act=w_act, w_dve_first=w_dve_first):
                            if w_dve_first is not None:
                                t.wait_ge(dve.sem, w_dve_first)
                            t.wait_ge(c_wg[par].sem, wg_cnt)
                            t.wait_ge(act.sem, w_act)
                            for c in range(NH):
                                inst = t.matmul(
                                    psb[th], wbg[par][:, 128 * c:128 * (c + 1)],
                                    xlnT[:, c * T + 512 * th: c * T + 512 * (th + 1)],
                                    start=(c == 0), stop=(c == NH - 1),
                                    skip_group_check=True)
                                if c == NH - 1:
                                    inst.then_inc(pe.sem, 1)
                        E.on("tensor", mmg)

                        w_dve = dve.n
                        pe_up = pe.bump()
                        def mmu(t, a=a, th=th, par=par, wu_cnt=wu_cnt, w_dve=w_dve):
                            t.wait_ge(dve.sem, w_dve)
                            t.wait_ge(c_wu[par].sem, wu_cnt)
                            for c in range(NH):
                                inst = t.matmul(
                                    psb[2 + th], wbu[par][:, 128 * c:128 * (c + 1)],
                                    xlnT[:, c * T + 512 * th: c * T + 512 * (th + 1)],
                                    start=(c == 0), stop=(c == NH - 1),
                                    skip_group_check=True)
                                if c == NH - 1:
                                    inst.then_inc(pe.sem, 1)
                        E.on("tensor", mmu)

                        w_vh_prev = dve.n
                        act_silu = act.bump()
                        def scs(sc, th=th, pe_gate=pe_gate, w_vh_prev=w_vh_prev):
                            sc.wait_ge(pe.sem, pe_gate)
                            sc.wait_ge(dve.sem, w_vh_prev)
                            sc.activation(sqscr[:, 512 * th:512 * (th + 1)],
                                          psb[th], AF.Silu).then_inc(act.sem, 1)
                        E.on("scalar", scs)

                        w_dve3 = dve.n
                        dve.bump()
                        def vh(v, a=a, th=th, act_silu=act_silu, pe_up=pe_up,
                               w_dve3=w_dve3):
                            v.wait_ge(pe.sem, pe_up)
                            v.wait_ge(act.sem, act_silu)
                            v.wait_ge(dve.sem, w_dve3)
                            v.tensor_tensor(
                                hact[:, a * T + 512 * th: a * T + 512 * (th + 1)],
                                sqscr[:, 512 * th:512 * (th + 1)],
                                psb[2 + th], ALU.mult).then_inc(dve.sem, 1)
                        E.on("vector", vh)
                    pe_hist.append(pe.n)
                    if tile_hook is not None:
                        tile_hook()
                dve_hact = dve.n
                if sub == "gateup":
                    continue

                prev_vres = None
                for og in range(4):
                    pe_dh = []
                    for a in range(na):
                        w_pe = pe.n if a < 4 else pe_dh[a - 4]
                        wd_cnt = c_wd[a % 4].bump(16)
                        def ldd(s, a=a, og=og, io=io, w_pe=w_pe):
                            if a == 0 and og == 0 and io == 0:
                                s.wait_ge(cc["wd"].sem, 1)
                            s.wait_ge(pe.sem, w_pe)
                            s.dma_start(
                                out=wbd[a % 4],
                                in_=wdT.ap()[io + 128 * a: io + 128 * (a + 1),
                                             512 * og: 512 * (og + 1)]
                            ).then_inc(c_wd[a % 4].sem, 16)
                        E.on("sync", ldd)

                        w_dve = (dve_hact if og == 0 else None) if a == 0 else None
                        pv_w = prev_vres if (a == 0 and og > 0) else None
                        pe.bump()
                        def mmd(t, a=a, og=og, na=na, wd_cnt=wd_cnt,
                                w_dve=w_dve, pv_w=pv_w):
                            if w_dve is not None:
                                t.wait_ge(dve.sem, w_dve)
                            t.wait_ge(c_wd[a % 4].sem, wd_cnt)
                            for tt in range(NT):
                                if pv_w is not None:
                                    t.wait_ge(dve.sem, pv_w[tt])
                                inst = t.matmul(
                                    B["dbank"][tt],
                                    hact[:, a * T + 128 * tt: a * T + 128 * (tt + 1)],
                                    wbd[a % 4],
                                    start=(a == 0), stop=(a == na - 1),
                                    skip_group_check=True)
                                if tt == NT - 1:
                                    inst.then_inc(pe.sem, 1)
                        E.on("tensor", mmd)
                        pe_dh.append(pe.n)
                    pe_down = pe.n

                    w_dve4 = dve.n
                    vres_seq = [dve.bump() for _ in range(NT)]
                    def vres(v, og=og, pe_down=pe_down, w_dve4=w_dve4,
                             vres_seq=tuple(vres_seq)):
                        v.wait_ge(pe.sem, pe_down)
                        v.wait_ge(dve.sem, w_dve4)
                        for tt in range(NT):
                            if tt > 0:
                                v.wait_ge(dve.sem, vres_seq[tt - 1])
                            v.scalar_tensor_tensor(
                                xres3[:, tt, 512 * og:512 * (og + 1)],
                                B["dbank"][tt], 0.5,
                                xres3[:, tt, 512 * og:512 * (og + 1)],
                                ALU.mult, ALU.add).then_inc(dve.sem, 1)
                    E.on("vector", vres)
                    prev_vres = vres_seq

        def make_ffn_bufs(ps, tag):
            wbg = [ps.enter_context(nc.sbuf_tensor(f"wbg{tag}{i}", [P, D], BF16)).ap()
                   for i in range(2)]
            wbu = [ps.enter_context(nc.sbuf_tensor(f"wbu{tag}{i}", [P, D], BF16)).ap()
                   for i in range(2)]
            wbd = [ps.enter_context(nc.sbuf_tensor(f"wbd{tag}{i}", [P, 512], BF16)).ap()
                   for i in range(4)]
            bt = [ps.enter_context(nc.psum_tensor(f"bt{tag}{i}", [P, 1024], BF16)).ap()
                  for i in range(2)]
            psb = [ps.enter_context(nc.psum_tensor(f"b{tag}{i}", [P, 512], F32)).ap()
                   for i in range(6)]
            dbank = psb[:6] + [bt[0].bitcast(F32), bt[1].bitcast(F32)]
            return dict(wbg=wbg, wbu=wbu, wbd=wbd, psb=psb, dbank=dbank), bt

        with ExitStack() as p1:
            B1, bt1 = make_ffn_bufs(p1, "f1_")
            blk = p1.enter_context(nc.Block())
            E.set_block(blk)
            gate_phase({k: v for k, v in bar_p0.items()
                        if k not in ("c_cast",)})
            sub = {1: "ln", 2: "gateup"}.get(stop, "full")
            cast_chunks = [(m, k) for m in "qkvo" for k in range(8)]
            cast_skip = [4]
            def cast_hook():
                if cast_skip[0] > 0:
                    cast_skip[0] -= 1
                    return
                if not cast_chunks:
                    return
                m, k = cast_chunks.pop(0)
                w_pe = pe.n
                c_cast.bump(16)
                def gc(g, m=m, k=k, w_pe=w_pe):
                    g.wait_ge(pe.sem, w_pe)
                    g.dma_start(out=wnat[m].ap()[256 * k:256 * (k + 1), :],
                                in_=w_in[m][256 * k:256 * (k + 1), :]
                                ).then_inc(c_cast.sem, 16)
                E.on("gpsimd", gc)
            emit_ffn(B1, bt1, sub=sub, tile_hook=cast_hook)
            while cast_chunks:
                cast_hook()
            cast_all = c_cast.n
            bar_p1 = snap()
            E.flush()

        if stop <= 3:
            with nc.Block() as blk:
                E.set_block(blk)
                if dumpln:
                    w1 = mkwaits(bar_p1)
                    tgt = c_shst.bump(16)
                    def gd(g):
                        w1(g)
                        g.dma_start(out=dln, in_=xlnT).then_inc(c_shst.sem, 16)
                        g.wait_ge(c_shst.sem, tgt)
                    E.on("gpsimd", gd)
                if dumph:
                    w1 = mkwaits(bar_p1)
                    tgt = c_shst.bump(16)
                    def gd2(g):
                        w1(g)
                        g.dma_start(out=dh, in_=cbuf).then_inc(c_shst.sem, 16)
                        g.wait_ge(c_shst.sem, tgt)
                    E.on("gpsimd", gd2)
                emit_store(bar_p1)
                E.flush()
            return nc

        with ExitStack() as p2:
            wpb = [p2.enter_context(nc.sbuf_tensor(f"wpb{i}", [P, D], BF16)).ap()
                   for i in range(2)]
            wvb = [p2.enter_context(nc.sbuf_tensor(f"wvb{i}", [P, 512], BF16)).ap()
                   for i in range(4)]
            kstg = [p2.enter_context(nc.sbuf_tensor(f"kstg{i}", [P, 1024], BF16)).ap()
                    for i in range(2)]
            vstg = [p2.enter_context(nc.sbuf_tensor(f"vstg{i}", [P, 4096], BF16)).ap()
                    for i in range(2)]
            bt2 = [p2.enter_context(nc.psum_tensor(f"bt2_{i}", [P, 1024], BF16)).ap()
                   for i in range(2)]
            ps2 = [p2.enter_context(nc.psum_tensor(f"b2_{i}", [P, 512], F32)).ap()
                   for i in range(6)]
            vbank = ps2[:6] + [bt2[0].bitcast(F32), bt2[1].bitcast(F32)]
            blk = p2.enter_context(nc.Block())
            E.set_block(blk)
            gate_phase(bar_p1)

            emit_ln_all(bt2)
            ln2_done = dve.n
            ln2_act = act.n

            cast_waited = [False]

            def proj_oxT(wname, bname, dst_fn, first_wait=None):
                pe_h = []
                for oc in range(NH):
                    par = oc % 2
                    w_pe = pe.n if oc < 2 else pe_h[oc - 2]
                    wp_cnt = c_wp[par].bump(16)
                    need_cast = not cast_waited[0]
                    cast_waited[0] = True
                    def ldp(s, oc=oc, par=par, wname=wname, w_pe=w_pe,
                            need_cast=need_cast):
                        if need_cast:
                            s.wait_ge(c_cast.sem, cast_all)
                        s.wait_ge(pe.sem, w_pe)
                        s.dma_start(
                            out=wpb[par].rearrange("p (c o) -> p c o", o=P),
                            in_=wnat[wname].ap()[128 * oc:128 * (oc + 1), :],
                            transpose=True).then_inc(c_wp[par].sem, 16)
                    E.on("sync", ldp)
                    for th in range(2):
                        bb = ps2[2 * (oc % 2) + th]
                        fw = first_wait if (oc == 0 and th == 0) else None
                        w_dve = dve.n
                        pe_p = pe.bump()
                        def mmp(t, oc=oc, th=th, par=par, bb=bb, wp_cnt=wp_cnt,
                                fw=fw, w_dve=w_dve, bname=bname):
                            if fw is not None:
                                fw(t)
                            t.wait_ge(c_wp[par].sem, wp_cnt)
                            t.wait_ge(dve.sem, w_dve)
                            for c in range(NH):
                                t.matmul(bb, wpb[par][:, 128 * c:128 * (c + 1)],
                                         xlnT[:, c * T + 512 * th:
                                              c * T + 512 * (th + 1)],
                                         start=(c == 0), stop=False,
                                         skip_group_check=True)
                            t.matmul(bb, brow[bname][0:1, 128 * oc:128 * (oc + 1)],
                                     ones_bf[0:1, 0:512], start=False, stop=True,
                                     skip_group_check=True).then_inc(pe.sem, 1)
                        E.on("tensor", mmp)
                        dst_fn(oc, th, bb, pe_p)
                    pe_h.append(pe.n)

            kst_done = {}
            def k_dst(oc, th, bb, pe_p):
                par = oc % 2
                w_st = c_st2.n if oc >= 2 else None
                w_dve = dve.n
                dve_k = dve.bump()
                def evk(v, th=th, bb=bb, pe_p=pe_p, w_st=w_st, w_dve=w_dve,
                        par=par):
                    v.wait_ge(pe.sem, pe_p)
                    if w_st is not None:
                        v.wait_ge(c_st2.sem, w_st)
                    v.wait_ge(dve.sem, w_dve)
                    v.tensor_copy(kstg[par][:, 512 * th:512 * (th + 1)],
                                  bb).then_inc(dve.sem, 1)
                E.on("vector", evk)
                if th == 1:
                    kst_done[oc] = c_st2.bump(16)
                    def stk(g, oc=oc, par=par, dve_k=dve_k):
                        g.wait_ge(dve.sem, dve_k)
                        g.dma_start(
                            out=kT_loc.ap()[128 * oc:128 * (oc + 1), :],
                            in_=kstg[par]).then_inc(c_st2.sem, 16)
                    E.on("gpsimd", stk)

            def _kfw(t):
                t.wait_ge(dve.sem, ln2_done)
                t.wait_ge(act.sem, ln2_act)
            proj_oxT("k", "k", k_dst, first_wait=_kfw)

            st_k = c_st2.n
            cc["kt"].bump()
            def gagk(g, st_k=st_k):
                g.wait_ge(c_st2.sem, st_k)
                g.collective_compute(
                    "AllGather", ALU.bypass, replica_groups=[list(range(NCORE))],
                    ins=[kT_loc.ap()], outs=[kT_all.ap()]).then_inc(cc["kt"].sem, 1)
            E.on("gpsimd", gagk)

            vst_done = {}
            for og in range(4):
                pe_hv = []
                for c in range(NH):
                    w_pe = pe.n if c < 4 else pe_hv[c - 4]
                    wv_cnt = c_wv[c % 4].bump(16)
                    def ldv(s, og=og, c=c, w_pe=w_pe):
                        s.wait_ge(pe.sem, w_pe)
                        s.dma_start(
                            out=wvb[c % 4],
                            in_=wnat["v"].ap()[512 * og:512 * (og + 1),
                                               128 * c:128 * (c + 1)],
                            transpose=True).then_inc(c_wv[c % 4].sem, 16)
                    E.on("sync", ldv)
                    w_dve = dve.n if c == 0 else None
                    pe.bump()
                    def mmv(t, og=og, c=c, wv_cnt=wv_cnt, w_dve=w_dve):
                        if w_dve is not None:
                            t.wait_ge(dve.sem, w_dve)
                        t.wait_ge(c_wv[c % 4].sem, wv_cnt)
                        for tt in range(NT):
                            inst = t.matmul(
                                vbank[tt],
                                xlnT[:, c * T + 128 * tt: c * T + 128 * (tt + 1)],
                                wvb[c % 4],
                                start=(c == 0), stop=False, skip_group_check=True)
                            if tt == NT - 1:
                                inst.then_inc(pe.sem, 1)
                    E.on("tensor", mmv)
                    pe_hv.append(pe.n)
                pe_v = pe.bump()
                def mmvb(t, og=og):
                    for tt in range(NT):
                        inst = t.matmul(vbank[tt], ones_bf[0:1, 0:128],
                                        brow["v"][0:1, 512 * og:512 * (og + 1)],
                                        start=False, stop=True,
                                        skip_group_check=True)
                        if tt == NT - 1:
                            inst.then_inc(pe.sem, 1)
                E.on("tensor", mmvb)
                par = og % 2
                w_st = c_st2.n if og >= 2 else None
                for tt in range(NT):
                    w_dve = dve.n
                    dve_v2 = dve.bump()
                    def evv(v, tt=tt, pe_v=pe_v, w_st=(w_st if tt == 0 else None),
                            w_dve=w_dve, par=par):
                        v.wait_ge(pe.sem, pe_v)
                        if w_st is not None:
                            v.wait_ge(c_st2.sem, w_st)
                        v.wait_ge(dve.sem, w_dve)
                        v.tensor_copy(vstg[par][:, 512 * tt:512 * (tt + 1)],
                                      vbank[tt]).then_inc(dve.sem, 1)
                    E.on("vector", evv)
                vst_done[og] = c_st2.bump(16)
                def stv(g, og=og, par=par, dve_v2=dve_v2):
                    g.wait_ge(dve.sem, dve_v2)
                    g.dma_start(
                        out=v_loc.ap().rearrange("(tt p) d -> p tt d", p=P)
                            [:, :, 512 * og:512 * (og + 1)],
                        in_=vstg[par].rearrange("p (tt d) -> p tt d", d=512)
                    ).then_inc(c_st2.sem, 16)
                E.on("gpsimd", stv)

            st_v = c_st2.n
            cc["v"].bump()
            def gagv(g, st_v=st_v):
                g.wait_ge(c_st2.sem, st_v)
                g.collective_compute(
                    "AllGather", ALU.bypass, replica_groups=[list(range(NCORE))],
                    ins=[v_loc.ap()], outs=[v_all.ap()]).then_inc(cc["v"].sem, 1)
            E.on("gpsimd", gagv)

            def q_dst(oc, th, bb, pe_p):
                w_dve = dve.n
                dve.bump()
                def evq(v, oc=oc, th=th, bb=bb, pe_p=pe_p, w_dve=w_dve):
                    v.wait_ge(pe.sem, pe_p)
                    v.wait_ge(dve.sem, w_dve)
                    v.tensor_copy(
                        qT[:, oc * T + 512 * th: oc * T + 512 * (th + 1)],
                        bb).then_inc(dve.sem, 1)
                E.on("vector", evq)

            proj_oxT("q", "q", q_dst)
            bar_p2 = snap()
            E.flush()

        if stop == 4:
            with nc.Block() as blk:
                E.set_block(blk)
                emit_store(bar_p2)
                E.flush()
            return nc

        with ExitStack() as p3:
            ksb = [p3.enter_context(nc.sbuf_tensor(f"ksb{i}", [P, 2048], BF16)).ap()
                   for i in range(2)]
            vsb = [p3.enter_context(nc.sbuf_tensor(f"vsb{i}", [P, 16 * 129], BF16)).ap()
                   for i in range(2)]
            pT = [p3.enter_context(nc.sbuf_tensor(f"pT{i}", [P, T], BF16)).ap()
                  for i in range(4)]
            attn_sb = p3.enter_context(nc.sbuf_tensor("attn_sb", [P, T], BF16)).ap()
            rcp = p3.enter_context(nc.sbuf_tensor("rcp", [P, 8], F32)).ap()
            ps_s = [p3.enter_context(nc.psum_tensor(f"ps_s{i}", [P, T], F32)).ap()
                    for i in range(2)]
            ps_a = [p3.enter_context(nc.psum_tensor("ps_a0", [P, 387], F32)).ap(),
                    p3.enter_context(nc.psum_tensor("ps_a1", [P, 387], F32)).ap(),
                    p3.enter_context(nc.psum_tensor("ps_a2", [P, 258], F32)).ap()]
            ps_tr = p3.enter_context(nc.psum_tensor("ps_tr", [P, 512], BF16)).ap()
            blk = p3.enter_context(nc.Block())
            E.set_block(blk)
            gate_phase(bar_p2)

            def acc_slice(j):
                return ps_a[j // 3][:, 129 * (j % 3):129 * (j % 3) + 129]

            def acc_out(j):
                return ps_a[j // 3][:, 129 * (j % 3):129 * (j % 3) + 128]

            def acc_sum(j):
                return ps_a[j // 3][:, 129 * (j % 3) + 128:129 * (j % 3) + 129]

            kT4 = kT_all.ap().rearrange("(r p) (c f) -> p r c f", p=D, f=P)
            v3 = v_all.ap().rearrange("(j p) d -> p j d", p=P)

            pool_ones2 = pool.bump(2)
            def gones(g):
                for par in range(2):
                    g.memset(
                        vsb[par].rearrange("p (j f) -> p j f", f=129)[:, :, 128:129],
                        1.0).then_inc(pool.sem, 1)
            E.on("gpsimd", gones)

            sc_cnt = {}
            pv_cnt = {}
            exp_cnt = {}
            kload = {}
            vload = {}
            NB = NH * 4

            def emit_load(gb):
                h, b = gb // 4, gb % 4
                w_sc = sc_cnt.get(16 * (gb - 2) + 15)
                w_pv = pv_cnt.get(16 * (gb - 2) + 15)
                kload[gb] = c_k[gb % 2].bump(16)
                vload[gb] = c_v[gb % 2].bump(16)
                def lkv(s, gb=gb, h=h, b=b, w_sc=w_sc, w_pv=w_pv):
                    if gb < 2:
                        s.wait_ge(cc["kt"].sem, 1)
                        s.wait_ge(cc["v"].sem, 1)
                    if w_sc is not None:
                        s.wait_ge(pe.sem, w_sc)
                    s.dma_start(
                        out=ksb[gb % 2].rearrange("p (r c f) -> p r c f", r=2, f=P),
                        in_=kT4[128 * h:128 * (h + 1), 2 * b:2 * b + 2, :, :]
                    ).then_inc(c_k[gb % 2].sem, 16)
                    if w_pv is not None:
                        s.wait_ge(pe.sem, w_pv)
                    if gb == 0:
                        s.wait_ge(pool.sem, pool_ones2)
                    s.dma_start(
                        out=vsb[gb % 2].rearrange("p (j f) -> p j f", f=129)
                            [:, :, 0:128],
                        in_=v3[:, 16 * b:16 * (b + 1), 128 * h:128 * (h + 1)]
                    ).then_inc(c_v[gb % 2].sem, 16)
                E.on("sync", lkv)

            def emit_scores(gc):
                gb, kc = gc // 16, gc % 16
                h = gb // 4
                pp = gc % 2
                w_exp = exp_cnt.get(gc - 2)
                w_k = kload[gb]
                sc_cnt[gc] = pe.bump()
                def tsc(t, gb=gb, kc=kc, h=h, pp=pp, w_exp=w_exp, w_k=w_k):
                    t.wait_ge(c_k[gb % 2].sem, w_k)
                    if w_exp is not None:
                        t.wait_ge(act.sem, w_exp)
                    lhs = ksb[gb % 2][:, 128 * kc:128 * (kc + 1)]
                    t.matmul(ps_s[pp][:, 0:512], lhs, qT[:, h * T: h * T + 512],
                             start=True, stop=True, skip_group_check=True)
                    t.matmul(ps_s[pp][:, 512:1024], lhs,
                             qT[:, h * T + 512: h * T + 1024],
                             start=True, stop=True,
                             skip_group_check=True).then_inc(pe.sem, 1)
                E.on("tensor", tsc)

            def emit_exp(gc):
                pp = gc % 2
                p4_ = gc % 4
                w_sc = sc_cnt[gc]
                w_pv = pv_cnt.get(gc - 4)
                exp_cnt[gc] = act.bump()
                def sce(sc, pp=pp, p4_=p4_, w_sc=w_sc, w_pv=w_pv):
                    sc.wait_ge(pe.sem, w_sc)
                    if w_pv is not None:
                        sc.wait_ge(pe.sem, w_pv)
                    sc.activation(pT[p4_], ps_s[pp], AF.Exp,
                                  scale=SCALE).then_inc(act.sem, 1)
                E.on("scalar", sce)

            def emit_pv(gc, dve_prev_norm):
                gb, kc = gc // 16, gc % 16
                kin = gc % 64
                pp = gc % 2
                w_exp = exp_cnt[gc]
                w_v = vload[gb]
                pv_cnt[gc] = pe.bump()
                p4_ = gc % 4
                def tpv(t, gb=gb, kc=kc, kin=kin, p4_=p4_, w_exp=w_exp,
                        w_v=w_v, dve_prev_norm=dve_prev_norm):
                    t.wait_ge(act.sem, w_exp)
                    t.wait_ge(c_v[gb % 2].sem, w_v)
                    for j in range(8):
                        if kin == 0 and dve_prev_norm is not None and j % 3 == 0:
                            t.wait_ge(dve.sem, dve_prev_norm[min(j + 2, 7)])
                        inst = t.matmul(
                            acc_slice(j), pT[p4_][:, 128 * j:128 * (j + 1)],
                            vsb[gb % 2][:, 129 * kc:129 * kc + 129],
                            start=(kin == 0 and j % 3 == 0), stop=(kin == 63),
                            skip_group_check=True)
                        if j == 7:
                            inst.then_inc(pe.sem, 1)
                E.on("tensor", tpv)

            tr_state = {"pe_tr": None, "prev_ev": None, "dve_norm": None}

            def flash_head_end(h, pv_last):
                w_pe_tr = tr_state["pe_tr"]
                w_dve = dve.n
                rec_seq = [dve.bump() for _ in range(8)]
                for _ in range(8):
                    dve.bump()
                norm_seq = [rec_seq[7] + j + 1 for j in range(8)]
                def vnorm(v, h=h, pv_last=pv_last, w_pe_tr=w_pe_tr, w_dve=w_dve,
                          rec_seq=tuple(rec_seq)):
                    v.wait_ge(pe.sem, pv_last)
                    if w_pe_tr is not None:
                        v.wait_ge(pe.sem, w_pe_tr)
                    v.wait_ge(dve.sem, w_dve)
                    for j in range(8):
                        if j > 0:
                            v.wait_ge(dve.sem, rec_seq[j - 1])
                        v.reciprocal(rcp[:, j:j + 1], acc_sum(j)).then_inc(dve.sem, 1)
                    v.wait_ge(dve.sem, rec_seq[7])
                    for j in range(8):
                        v.tensor_scalar(
                            attn_sb[:, 128 * j:128 * (j + 1)], acc_out(j),
                            rcp[:, j:j + 1], None, ALU.mult).then_inc(dve.sem, 1)
                E.on("vector", vnorm)
                tr_state["dve_norm"] = norm_seq
                dn = dve.n
                for g4 in range(2):
                    w_prev = tr_state["prev_ev"]
                    pe_tr = pe.bump()
                    def ttr(t, g4=g4, dn=dn, w_prev=w_prev):
                        t.wait_ge(dve.sem, dn)
                        if w_prev is not None:
                            t.wait_ge(dve.sem, w_prev)
                        for i in range(4):
                            j = 4 * g4 + i
                            inst = t.transpose(ps_tr[:, 128 * i:128 * (i + 1)],
                                               attn_sb[:, 128 * j:128 * (j + 1)],
                                               ident)
                            if i == 3:
                                inst.then_inc(pe.sem, 1)
                    E.on("tensor", ttr)
                    w_dve5 = dve.n
                    tr_state["prev_ev"] = dve.bump()
                    def evt(v, h=h, g4=g4, pe_tr=pe_tr, w_dve5=w_dve5):
                        v.wait_ge(pe.sem, pe_tr)
                        v.wait_ge(dve.sem, w_dve5)
                        v.tensor_copy(
                            attnT[:, h * T + 512 * g4: h * T + 512 * (g4 + 1)],
                            ps_tr).then_inc(dve.sem, 1)
                    E.on("vector", evt)
                tr_state["pe_tr"] = pe.n

            emit_load(0)
            emit_load(1)
            for h in range(NH):
                base = 64 * h
                for kin in range(64):
                    gc = base + kin
                    emit_scores(gc)
                    if gc > 0:
                        emit_exp(gc - 1)
                        emit_pv(gc - 1, tr_state["dve_norm"]
                                if (gc - 1) % 64 == 0 else None)
                        if (gc - 1) % 64 == 63:
                            flash_head_end(h - 1, pv_cnt[gc - 1])
                        if (gc - 1) % 16 == 15 and ((gc - 1) // 16) + 2 < NB:
                            emit_load((gc - 1) // 16 + 2)
            emit_exp(64 * NH - 1)
            emit_pv(64 * NH - 1, None)
            flash_head_end(NH - 1, pv_cnt[64 * NH - 1])

            bar_p3 = snap()
            E.flush()

        if stop == 5:
            with nc.Block() as blk:
                E.set_block(blk)
                emit_store(bar_p3)
                E.flush()
            return nc

        with ExitStack() as p4:
            wob = [p4.enter_context(nc.sbuf_tensor(f"wob{i}", [P, 512], BF16)).ap()
                   for i in range(4)]
            B4 = [p4.enter_context(nc.psum_tensor(f"b4_{i}", [P, 512], F32)).ap()
                  for i in range(8)]
            blk = p4.enter_context(nc.Block())
            E.set_block(blk)
            gate_phase(bar_p3)

            for og in range(4):
                pe_ho = []
                for c in range(NH):
                    w_pe = pe.n if c < 4 else pe_ho[c - 4]
                    wo_cnt = c_wv[c % 4].bump(16)
                    def ldo(s, og=og, c=c, w_pe=w_pe):
                        s.wait_ge(pe.sem, w_pe)
                        s.dma_start(
                            out=wob[c % 4],
                            in_=wnat["o"].ap()[512 * og:512 * (og + 1),
                                               128 * c:128 * (c + 1)],
                            transpose=True).then_inc(c_wv[c % 4].sem, 16)
                    E.on("sync", ldo)
                    w_dve = dve.n if c == 0 else None
                    pe.bump()
                    def mmo(t, og=og, c=c, wo_cnt=wo_cnt, w_dve=w_dve):
                        if w_dve is not None:
                            t.wait_ge(dve.sem, w_dve)
                        t.wait_ge(c_wv[c % 4].sem, wo_cnt)
                        for tt in range(NT):
                            inst = t.matmul(
                                B4[tt],
                                attnT[:, c * T + 128 * tt: c * T + 128 * (tt + 1)],
                                wob[c % 4],
                                start=(c == 0), stop=False, skip_group_check=True)
                            if tt == NT - 1:
                                inst.then_inc(pe.sem, 1)
                    E.on("tensor", mmo)
                    pe_ho.append(pe.n)
                pe_o = pe.bump()
                def mmob(t, og=og):
                    for tt in range(NT):
                        inst = t.matmul(B4[tt], ones_bf[0:1, 0:128],
                                        brow["o"][0:1, 512 * og:512 * (og + 1)],
                                        start=False, stop=True,
                                        skip_group_check=True)
                        if tt == NT - 1:
                            inst.then_inc(pe.sem, 1)
                E.on("tensor", mmob)
                w_dve6 = dve.n
                vro_seq = [dve.bump() for _ in range(NT)]
                def vro(v, og=og, pe_o=pe_o, w_dve6=w_dve6, vro_seq=tuple(vro_seq)):
                    v.wait_ge(pe.sem, pe_o)
                    v.wait_ge(dve.sem, w_dve6)
                    for tt in range(NT):
                        if tt > 0:
                            v.wait_ge(dve.sem, vro_seq[tt - 1])
                        v.scalar_tensor_tensor(
                            xres3[:, tt, 512 * og:512 * (og + 1)],
                            B4[tt], 1.0,
                            xres3[:, tt, 512 * og:512 * (og + 1)],
                            ALU.mult, ALU.add).then_inc(dve.sem, 1)
                E.on("vector", vro)
            bar_p4 = snap()
            E.flush()

        if stop == 6:
            with nc.Block() as blk:
                E.set_block(blk)
                emit_store(bar_p4)
                E.flush()
            return nc

        with ExitStack() as p5:
            B5, bt5 = make_ffn_bufs(p5, "f5_")
            blk = p5.enter_context(nc.Block())
            E.set_block(blk)
            gate_phase(bar_p4)
            emit_ffn(B5, bt5)
            bar_p5 = snap()
            emit_store(bar_p5)
            E.flush()

    return nc


_NC_CACHE = {}


def _get_nc(stop_stage="ffn2"):
    if stop_stage not in _NC_CACHE:
        _NC_CACHE[stop_stage] = build_nc(stop_stage)
    return _NC_CACHE[stop_stage]


def kernel(L0_attn_hidden_states, L0_attn_wq, L0_attn_bq, L0_attn_wk, L0_attn_bk,
           L0_attn_wv, L0_attn_bv, L0_attn_wo, L0_attn_bo, L0_ffn_lut,
           L0_ffn_gate_idx, L0_ffn_up_idx, L0_ffn_down_idx, stop_stage="ffn2"):
    x = np.ascontiguousarray(np.asarray(L0_attn_hidden_states, dtype=np.float32)
                             .reshape(SEQ, D))
    arrs = dict(
        wq=np.ascontiguousarray(np.asarray(L0_attn_wq, dtype=np.float32)),
        wk=np.ascontiguousarray(np.asarray(L0_attn_wk, dtype=np.float32)),
        wv=np.ascontiguousarray(np.asarray(L0_attn_wv, dtype=np.float32)),
        wo=np.ascontiguousarray(np.asarray(L0_attn_wo, dtype=np.float32)),
        bq=np.ascontiguousarray(np.asarray(L0_attn_bq, dtype=np.float32)),
        bk=np.ascontiguousarray(np.asarray(L0_attn_bk, dtype=np.float32)),
        bv=np.ascontiguousarray(np.asarray(L0_attn_bv, dtype=np.float32)),
        bo=np.ascontiguousarray(np.asarray(L0_attn_bo, dtype=np.float32)),
        lut=np.ascontiguousarray(np.asarray(L0_ffn_lut, dtype=np.float32)),
    )
    gidx = np.asarray(L0_ffn_gate_idx, dtype=np.int32)
    uidx = np.asarray(L0_ffn_up_idx, dtype=np.int32)
    didx = np.asarray(L0_ffn_down_idx, dtype=np.int32)

    nc = _get_nc(stop_stage)
    in_maps = []
    for r in range(NCORE):
        m = dict(arrs)
        m["x"] = np.ascontiguousarray(x[T * r:T * (r + 1)])
        m["gidx"] = np.ascontiguousarray(gidx[:, GSH * r:GSH * (r + 1)])
        m["uidx"] = np.ascontiguousarray(uidx[:, GSH * r:GSH * (r + 1)])
        m["didx"] = np.ascontiguousarray(didx[:, DSH * r:DSH * (r + 1)])
        in_maps.append(m)
    res = run_bass_kernel_spmd(nc, in_maps, core_ids=list(range(NCORE)))
    out = np.concatenate([res.results[r]["y"] for r in range(NCORE)], axis=0)
    return out.reshape(1, SEQ, D)
